# version 01 (final)
# baseline (speedup 1.0000x reference)
import numpy as np
import ml_dtypes
from contextlib import ExitStack

import concourse.bass as bass
import concourse.mybir as mybir
from concourse.bass_utils import run_bass_kernel_spmd

F32 = mybir.dt.float32
BF16 = mybir.dt.bfloat16
AF = mybir.ActivationFunctionType
ALU = mybir.AluOpType
AX = mybir.AxisListType

NCORES = 8
SEQ = 8192
D = 1024
TPC = 4096

ENGINES = ("pe", "act", "dve", "pool", "sp")


def amax(*toks):
    best = {}

    def add(t):
        if t is None:
            return
        if isinstance(t, tuple) and len(t) == 2 and isinstance(t[0], str):
            s, v = t
            if best.get(s, -1) < v:
                best[s] = v
        else:
            for u in t:
                add(u)
    for t in toks:
        add(t)
    return tuple(best.items())




class KB:
    def __init__(self, nc, paranoid=False):
        self.paranoid = paranoid
        self.nc = nc
        self.q = {e: [] for e in ENGINES}
        self.sems = {}
        self.cnt = {}
        self.hwm = {e: {} for e in ENGINES}
        self._semnames = []

    def sem(self, name):
        if name not in self.cnt:
            self.cnt[name] = 0
            self._semnames.append(name)
        return name

    def alloc_sems(self, stack):
        for n in self._semnames:
            self.sems[n] = stack.enter_context(self.nc.semaphore(n))

    def wait(self, e, tok):
        if tok is None:
            return
        s, v = tok
        if v <= 0 or self.hwm[e].get(s, 0) >= v:
            return
        self.hwm[e][s] = v
        self.q[e].append(lambda eng, s=s, v=v: eng.wait_ge(self.sems[s], v))

    def op(self, e, fn, deps=(), tick=None):
        if self.paranoid == "dve":
            deps = tuple(deps) + (("dve", self.cnt.get("dve", 0)),)
        elif self.paranoid == "act":
            deps = tuple(deps) + (("act", self.cnt.get("act", 0)),)
        elif self.paranoid == "dveact":
            deps = tuple(deps) + (("dve", self.cnt.get("dve", 0)),
                                  ("act", self.cnt.get("act", 0)))
        elif self.paranoid == "pe":
            deps = tuple(deps) + (("pe", self.cnt.get("pe", 0)),)
        elif self.paranoid == "dma":
            deps = tuple(deps) + tuple(
                (s, self.cnt[s]) for s in self.cnt
                if s not in ("pe", "dve", "act"))
        elif self.paranoid:
            deps = tuple(deps) + tuple((s, self.cnt[s]) for s in self.cnt)
        for d in amax(*deps):
            self.wait(e, d)
        if tick is None:
            self.q[e].append(fn)
            return None
        if isinstance(tick, tuple):
            s, amt = tick
        else:
            s, amt = tick, 1
        s = self.sem(s)
        self.cnt[s] += amt
        v = self.cnt[s]
        self.q[e].append(
            lambda eng, fn=fn, s=s, amt=amt: fn(eng).then_inc(self.sems[s], amt))
        return (s, v)

    def tok(self, semname):
        s = self.sem(semname)
        return (s, self.cnt[s])

    def dma(self, e, out, in_, deps=(), sem="dma", **kw):
        return self.op(e, lambda eng: eng.dma_start(out=out, in_=in_, **kw),
                       deps=deps, tick=(self.sem(sem), 16))

    def finalize(self, block):
        emitters = {"pe": block.tensor, "act": block.scalar,
                    "dve": block.vector, "pool": block.gpsimd, "sp": block.sync}
        for e in ENGINES:
            items = self.q[e]
            if not items:
                continue

            def body(eng, items=items):
                for fn in items:
                    fn(eng)
            emitters[e](body)


def grouped(ap2d, g):
    return ap2d.rearrange("p (g k) -> p g k", g=g)


def e_mm_acc(kb, psum_ap, pairs, deps=()):
    n = len(pairs)
    tok = None
    for i, (l, r) in enumerate(pairs):
        tok = kb.op("pe", lambda t, l=l, r=r, i=i, n=n: t.matmul(
            psum_ap, l, r, start=(i == 0), stop=(i == n - 1)),
            deps=deps if i == 0 else (), tick="pe" if i == n - 1 else None)
    return tok


def e_transpose4(kb, srcs, ident, pbank, dst3, deps, bank_free, evac_war=None):
    t = None
    for c, s in enumerate(srcs):
        t = kb.op("pe", lambda te, s=s, c=c: te.transpose(
            pbank[:, 128 * c:128 * (c + 1)], s, ident),
            deps=(amax(*deps) + amax(bank_free)) if c == 0 else (),
            tick="pe" if c == len(srcs) - 1 else None)
    src3 = grouped(pbank[:, 0:128 * len(srcs)], len(srcs))
    return kb.op("act", lambda e: e.copy(dst3, src3),
                 deps=amax(t, evac_war), tick="act")


def e_layernorm(kb, x_ap, n, out_f32, sq_scr, stats, scol, eps_ap, deps,
                out_bf=None, gb=None, bb=None, war_out=None, war_sq=None):
    s_sum = stats[:, scol:scol + 1]
    s_std = stats[:, scol + 1:scol + 2]
    s_rstd = stats[:, scol + 2:scol + 3]
    t1 = kb.op("dve", lambda v: v.reduce_sum(s_sum, x_ap, axis=AX.X),
               deps=deps, tick="dve")
    t1b = kb.op("dve", lambda v: v.tensor_scalar_mul(s_sum, s_sum, 1.0 / n),
                deps=(t1,), tick="dve")
    t2 = kb.op("dve", lambda v: v.tensor_scalar(
        out_f32, x_ap, s_sum, None, ALU.subtract),
        deps=amax(t1b, war_out), tick="dve")
    t3 = kb.op("act", lambda s: s.activation(
        sq_scr, out_f32, AF.Square, accum_out=s_std),
        deps=amax(t2, war_sq), tick="act")
    t4 = kb.op("act", lambda s: s.activation(
        s_std, s_std, AF.Sqrt, scale=1.0 / n, bias=eps_ap[0]),
        deps=amax(t3, eps_ap[1]), tick="act")
    t5 = kb.op("dve", lambda v: v.reciprocal(s_rstd, s_std), deps=(t4,),
               tick="dve")
    t6 = kb.op("dve", lambda v: v.tensor_scalar(
        out_f32, out_f32, s_rstd, None, ALU.mult), deps=(t5,), tick="dve")
    if gb is not None:
        t6 = kb.op("dve", lambda v: v.tensor_tensor(
            out_f32, out_f32, gb, ALU.mult), deps=(t6,), tick="dve")
        t6 = kb.op("dve", lambda v: v.tensor_tensor(
            out_f32, out_f32, bb, ALU.add), deps=(t6,), tick="dve")
    tbf = None
    if out_bf is not None:
        tbf = kb.op("act", lambda s: s.copy(out_bf, out_f32), deps=(t6,),
                    tick="act")
    return t6, tbf


SRC = 2048
NHEADS = 16
HE = 64
VP = 2 * (HE + 1)
VAW = 16 * VP


def build_block1(nc):
    kb = KB(nc)
    dt = nc.dram_tensor
    xc = dt("xc", [TPC, D], F32, kind="ExternalInput").ap()
    sembT = dt("sembT", [D, SRC], F32, kind="ExternalInput").ap()
    vembT = dt("vembT", [D, SRC], F32, kind="ExternalInput").ap()
    Wq = dt("Wq", [D, D], F32, kind="ExternalInput").ap()
    Wk = dt("Wk", [D, D], F32, kind="ExternalInput").ap()
    Wv = dt("Wv", [D, D], F32, kind="ExternalInput").ap()
    Wo = dt("Wo", [D, D], F32, kind="ExternalInput").ap()
    biasr = dt("biasr", [128, 6 * D], F32, kind="ExternalInput").ap()
    ident_d = dt("ident", [128, 128], BF16, kind="ExternalInput").ap()
    xout = dt("xout", [TPC, D], F32, kind="ExternalOutput").ap()
    knT_d = dt("knT_d", [8, 128, SRC], BF16, kind="Internal").ap()
    vaug_d = dt("vaug_d", [8, 128, VAW], BF16, kind="Internal").ap()

    st = ExitStack()
    with st:
        def sb(name, shape, dtype):
            return st.enter_context(nc.sbuf_tensor(name, shape, dtype))

        def psum(name, shape, dtype=F32):
            return st.enter_context(nc.psum_tensor(name, shape, dtype))

        wq_s = sb("wq", [128, 8 * D], BF16)
        wo_s = sb("wo", [128, 8 * D], BF16)
        biasb = sb("biasb", [128, 6 * D], F32)
        ident = sb("ident_s", [128, 128], BF16)
        wAB = sb("wAB", [128, 8 * D], BF16)
        regB = sb("regB", [128, 17408], BF16)
        ctxT = sb("ctxT", [128, 2 * VAW], BF16)
        knbuf = sb("knbuf", [128, 2 * SRC], BF16)
        vbuf = sb("vbuf", [128, 2 * VAW], BF16)
        xb = sb("xb", [128, 4 * D], F32)
        tb = sb("tb", [128, 4 * D], F32)
        xib = sb("xib", [128, D], F32)
        scr = sb("scr", [128, 2 * D], F32)
        bcst = sb("bcst", [128, 512], F32)
        stats = sb("stats", [128, 64], F32)
        rden = sb("rden", [1, 512], F32)
        onesf = sb("onesf", [1, 128], F32)
        konst = sb("konst", [128, 2], F32)

        p_t = [psum("p_t0", [128, 512], BF16), psum("p_t1", [128, 512], BF16)]
        p_q = [psum("p_q0", [128, 512]), psum("p_q1", [128, 512])]
        p_s = [psum("p_s0", [128, 512]), psum("p_s1", [128, 512])]
        p_c = [psum("p_c0", [128, 512]), psum("p_c1", [128, 512])]

        bank_free = {}

        def wload(dst2d, src, deps=()):
            sap = src.rearrange("(c p) d -> p c d", p=128)
            return kb.dma("pool", grouped(dst2d, 8), sap, deps=deps,
                          sem="wload")

        wload(wq_s[:], Wq)
        wload(wAB[:], Wk)
        wload(wo_s[:], Wo)
        tok_bias = kb.dma("sp", biasb[:], biasr, sem="cload")
        tok_id = kb.dma("sp", ident[:], ident_d, sem="cload")
        tok_id = amax(tok_bias, tok_id)
        tok_bias = tok_id
        tok_semb = kb.dma("pool", grouped(regB[:, 0:8 * SRC], 8),
                          sembT.rearrange("(c p) s -> p c s", p=128),
                          sem="wload")
        tok_wq = tok_wk = tok_wo = tok_semb
        tok_ones = kb.op("dve", lambda v: v.memset(onesf[:], 1.0), tick="dve")
        kb.op("dve", lambda v: v.memset(konst[:, 0:1], 1e-5))
        tok_k = kb.op("dve", lambda v: v.memset(konst[:, 1:2], 1e-12),
                      tick="dve")
        eps5 = (konst[:, 0:1], tok_k)
        eps12 = (konst[:, 1:2], tok_k)

        def bias3(r, half):
            return grouped(biasb[:, r * D + half * 512:
                                 r * D + half * 512 + 512], 8)

        def elu_l2norm(dst3, psrc, b3, mm_tok, par, dst_war=None):
            a = scr[:, 1024 * par: 1024 * par + 512]
            b = scr[:, 1024 * par + 512: 1024 * par + 1024]
            a3 = grouped(a, 8)
            b3v = grouped(b, 8)
            sc = 32 + 16 * par
            t1 = kb.op("dve", lambda v: v.tensor_tensor(
                a3, grouped(psrc[:], 8), b3, ALU.add),
                deps=amax(mm_tok, tok_bias, dst_war), tick="dve")
            t2 = kb.op("dve", lambda v: v.tensor_scalar_min(b, a, 0.0),
                       deps=(t1,), tick="dve")
            t3 = kb.op("act", lambda s: s.activation(b, b, AF.Exp),
                       deps=(t2,), tick="act")
            t4 = kb.op("dve", lambda v: v.tensor_scalar_max(a, a, 0.0),
                       deps=(t1,), tick="dve")
            t5 = kb.op("dve", lambda v: v.scalar_tensor_tensor(
                a, a, -1.0, b, ALU.add, ALU.add), deps=amax(t3, t4),
                tick="dve")
            t6 = kb.op("dve", lambda v: v.tensor_tensor(
                b3v, a3, a3, ALU.mult), deps=(t5,), tick="dve")
            ssq = stats[:, sc:sc + 8]
            t7 = kb.op("dve", lambda v: v.reduce_sum(
                grouped(ssq, 8), b3v, axis=AX.X), deps=(t6,), tick="dve")
            t8 = kb.op("act", lambda s: s.activation(ssq, ssq, AF.Sqrt),
                       deps=(t7,), tick="act")
            t9 = kb.op("dve", lambda v: v.reciprocal(ssq, ssq), deps=(t8,),
                       tick="dve")
            rn3 = grouped(ssq, 8).broadcast_to([128, 8, HE])
            t10 = kb.op("dve", lambda v: v.tensor_tensor(
                dst3, a3, rn3, ALU.mult), deps=(t9,), tick="dve")
            return t10, t5

        KNB = 16384
        knb_war = {}
        for srt in range(16):
            for half in range(2):
                pq = p_q[half]
                pairs = [(regB[:, d2 * SRC + srt * 128:
                               d2 * SRC + srt * 128 + 128],
                          wAB[:, d2 * D + half * 512: d2 * D + half * 512 + 512])
                         for d2 in range(8)]
                mm = e_mm_acc(kb, pq[:], pairs,
                              deps=amax(tok_wk, tok_semb,
                                        bank_free.get(pq.name)))
                dst3 = grouped(regB[:, KNB + half * 512:KNB + half * 512 + 512], 8)
                tok_kn, pfree = elu_l2norm(dst3, pq, bias3(1, half), mm, half,
                                           dst_war=knb_war.get("ev%d" % half))
                bank_free[pq.name] = pfree
                pt = p_t[half]
                srcs = [regB[:, KNB + half * 512 + 128 * c:
                             KNB + half * 512 + 128 * (c + 1)]
                        for c in range(4)]
                dst = knT_d[half * 4:half * 4 + 4, :,
                            srt * 128:srt * 128 + 128].rearrange("c p k -> p c k")
                stg3 = grouped(ctxT[:, half * 2048:half * 2048 + 512], 4)
                ev = e_transpose4(kb, srcs, ident[:], pt, stg3,
                                  deps=(tok_kn, tok_id),
                                  bank_free=bank_free.get(pt.name),
                                  evac_war=knb_war.get("stg%d" % half))
                bank_free[pt.name] = ev
                knb_war["ev%d" % half] = ev
                st_tok = kb.dma("sp", dst, stg3, deps=(ev,), sem="knst")
                knb_war["stg%d" % half] = st_tok
        tok_knT_done = kb.tok("knst")
        pe_kn_done = kb.tok("pe")

        kb.dma("pool", grouped(regB[:, 0:8 * SRC], 8),
               vembT.rearrange("(c p) s -> p c s", p=128),
               deps=(pe_kn_done,), sem="wload")
        tok_wv = wload(wAB[:], Wv, deps=(pe_kn_done,))
        tok_vemb = tok_wv
        stgV_war = {}
        for h2 in range(8):
            stg = ctxT[:, (h2 % 2) * VAW:(h2 % 2) * VAW + VAW]
            on_tok = kb.op("dve", lambda v, stg=stg: v.memset(
                stg.rearrange("p (s h e) -> p s h e", s=16, h=2)[:, :, :,
                                                                HE:HE + 1],
                1.0), deps=amax(stgV_war.get(h2 % 2)), tick="dve")
            last_ev = None
            for srt in range(16):
                pq = p_q[srt % 2]
                pairs = [(regB[:, d2 * SRC + srt * 128:
                               d2 * SRC + srt * 128 + 128],
                          wAB[:, d2 * D + h2 * 128: d2 * D + h2 * 128 + 128])
                         for d2 in range(8)]
                mm = e_mm_acc(kb, pq[:, 0:128], pairs,
                              deps=amax(tok_wv, tok_vemb,
                                        bank_free.get(pq.name)))
                dst3 = stg[:, srt * VP:srt * VP + VP].rearrange(
                    "p (h e) -> p h e", h=2)[:, :, 0:HE]
                b3 = grouped(biasb[:, 2 * D + h2 * 128:2 * D + h2 * 128 + 128], 2)
                last_ev = kb.op("dve", lambda v, dst3=dst3, pq=pq, b3=b3:
                                v.tensor_tensor(dst3, grouped(pq[:, 0:128], 2),
                                                b3, ALU.add),
                                deps=amax(mm, on_tok, tok_bias), tick="dve")
                bank_free[pq.name] = last_ev
            st_tok = kb.dma("sp", vaug_d[h2], stg, deps=(last_ev,), sem="knst")
            stgV_war[h2 % 2] = st_tok
        tok_vaug_done = kb.tok("knst")
        pe_A2_done = kb.tok("pe")
        bank_free["ctxT"] = tok_vaug_done

        TT, QNT, TBC, QBC, EXC = 0, 4096, 8192, 10240, 12288
        x_last_read = [None] * 8
        tb_last_read = {}
        tbfc_war = {0: pe_A2_done, 1: pe_A2_done}
        qbfc_war = {}
        exc_war = {}
        xload = [None] * 8

        def xload_emit(b):
            src = xc[512 * b:512 * (b + 1), :].rearrange(
                "(a p) d -> p a d", p=128)
            xload[b] = kb.dma("sp", grouped(xb[:], 4), src,
                              deps=amax(x_last_read[b - 1] if b >= 1 else None),
                              sem="xload")

        xload_emit(0)
        for b in range(8):
            xl = xload[b]
            tT_ev = {}
            for i in range(4):
                x_ap = xb[:, i * D:(i + 1) * D]
                t_f = tb[:, i * D:(i + 1) * D]
                tbfc = regB[:, TBC + (i % 2) * D: TBC + (i % 2) * D + D]
                ln_tok, bf_tok = e_layernorm(
                    kb, x_ap, D, t_f, tbfc, stats, 4 * (i % 2), eps5,
                    deps=(xl,), out_bf=tbfc,
                    war_out=tb_last_read.get(i), war_sq=tbfc_war.get(i % 2))
                evs = []
                for g in range(2):
                    pt = p_t[g]
                    srcs = [tbfc[:, 128 * (4 * g + c):128 * (4 * g + c + 1)]
                            for c in range(4)]
                    base = TT + (4 * g) * 512 + i * 128
                    dst = bass.AP(regB, base, [[17408, 128], [512, 4], [1, 128]])
                    ev = e_transpose4(kb, srcs, ident[:], pt, dst,
                                      deps=(bf_tok, tok_id),
                                      bank_free=bank_free.get(pt.name))
                    bank_free[pt.name] = ev
                    evs.append(ev)
                tT_ev[i] = evs
                tbfc_war[i % 2] = evs[-1]
            qnT_ev = []
            for i in range(4):
                qbfc = regB[:, QBC + (i % 2) * D: QBC + (i % 2) * D + D]
                for half in range(2):
                    pq = p_q[half]
                    pairs = [(regB[:, TT + d2 * 512 + i * 128:
                                   TT + d2 * 512 + i * 128 + 128],
                              wq_s[:, d2 * D + half * 512:
                                   d2 * D + half * 512 + 512])
                             for d2 in range(8)]
                    mm = e_mm_acc(kb, pq[:], pairs,
                                  deps=amax(tok_wq, tT_ev[i][0], tT_ev[i][1],
                                            bank_free.get(pq.name)))
                    dst3 = grouped(qbfc[:, half * 512:half * 512 + 512], 8)
                    tok_qn, pfree = elu_l2norm(
                        dst3, pq, bias3(0, half), mm, i % 2,
                        dst_war=qbfc_war.get((i % 2, half), pe_A2_done))
                    bank_free[pq.name] = pfree
                    pt = p_t[half]
                    srcs = [qbfc[:, half * 512 + 128 * c:
                                 half * 512 + 128 * (c + 1)]
                            for c in range(4)]
                    base = QNT + (half * 4) * 512 + i * 128
                    dst = bass.AP(regB, base, [[17408, 128], [512, 4], [1, 128]])
                    ev = e_transpose4(kb, srcs, ident[:], pt, dst,
                                      deps=(tok_qn, tok_id),
                                      bank_free=bank_free.get(pt.name))
                    bank_free[pt.name] = ev
                    qbfc_war[(i % 2, half)] = ev
                    qnT_ev.append(ev)

            kn_ld = {}
            v_ld = {}

            def load_kn(h2):
                kn_ld[h2] = kb.dma(
                    "sp", knbuf[:, (h2 % 2) * SRC:(h2 % 2) * SRC + SRC],
                    knT_d[h2],
                    deps=amax(tok_knT_done, bank_free.get("knb%d" % (h2 % 2))),
                    sem="knl%d" % (h2 % 2))

            def load_v(h2):
                v_ld[h2] = kb.dma(
                    "sp", vbuf[:, (h2 % 2) * VAW:(h2 % 2) * VAW + VAW],
                    vaug_d[h2],
                    deps=amax(tok_vaug_done, bank_free.get("vb%d" % (h2 % 2))),
                    sem="vl%d" % (h2 % 2))

            load_kn(0)
            load_v(0)
            for h in range(NHEADS):
                h2, hh = h // 2, h % 2
                if hh == 0 and h2 + 1 < 8:
                    load_kn(h2 + 1)
                    load_v(h2 + 1)
                pc = p_c[h % 2]
                ctx_mm = None
                kn_last_mm = None
                for srt in range(16):
                    ps = p_s[srt % 2]
                    lhs = knbuf[hh * 64:hh * 64 + 64,
                                (h2 % 2) * SRC + srt * 128:
                                (h2 % 2) * SRC + srt * 128 + 128]
                    rhs = regB[hh * 64:hh * 64 + 64,
                               QNT + h2 * 512:QNT + h2 * 512 + 512]
                    mm_s = kb.op("pe", lambda t, ps=ps, lhs=lhs, rhs=rhs:
                                 t.matmul(ps[:], lhs, rhs, start=True,
                                          stop=True),
                                 deps=amax(*qnT_ev) + amax(
                                     kn_ld[h2], bank_free.get(ps.name)),
                                 tick="pe")
                    kn_last_mm = mm_s
                    exc = regB[:, EXC + (srt % 2) * 512:
                               EXC + (srt % 2) * 512 + 512]
                    ex = kb.op("act", lambda s, exc=exc, ps=ps: s.activation(
                        exc, ps[:], AF.Exp, scale=0.125),
                        deps=amax(mm_s, exc_war.get(srt % 2)), tick="act")
                    bank_free[ps.name] = ex
                    lhs_v = vbuf[:, (h2 % 2) * VAW + srt * VP + hh * (HE + 1):
                                 (h2 % 2) * VAW + srt * VP + hh * (HE + 1)
                                 + HE + 1]
                    ctx_mm = kb.op(
                        "pe", lambda t, pc=pc, lhs_v=lhs_v, exc=exc, srt=srt:
                        t.matmul(pc[0:HE + 1, :], lhs_v, exc,
                                 start=(srt == 0), stop=(srt == 15)),
                        deps=amax(ex, v_ld[h2],
                                  bank_free.get(pc.name) if srt == 0 else None),
                        tick="pe" if srt == 15 else None)
                    exc_war[srt % 2] = ctx_mm
                if hh == 1:
                    bank_free["knb%d" % (h2 % 2)] = kn_last_mm
                    bank_free["vb%d" % (h2 % 2)] = ctx_mm
                t_r = kb.op("dve", lambda v, pc=pc: v.reciprocal(
                    rden[0:1, :], pc[HE:HE + 1, :]),
                    deps=amax(ctx_mm, bank_free.get("rden")), tick="dve")
                psb = p_s[h % 2]
                mm_b = kb.op("pe", lambda t, psb=psb: t.matmul(
                    psb[0:HE, :], onesf[0:1, 0:HE], rden[0:1, :],
                    start=True, stop=True),
                    deps=amax(t_r, tok_ones, bank_free.get(psb.name)),
                    tick="pe")
                bank_free["rden"] = mm_b
                ev_b = kb.op("act", lambda s, psb=psb: s.copy(
                    bcst[0:HE, :], psb[0:HE, :]),
                    deps=amax(mm_b, bank_free.get("bcst")), tick="act")
                bank_free[psb.name] = ev_b
                dst = ctxT[hh * 64:hh * 64 + 64, h2 * 512:h2 * 512 + 512]
                t_n = kb.op("dve", lambda v, dst=dst, pc=pc: v.tensor_tensor(
                    dst, pc[0:HE, :], bcst[0:HE, :], ALU.mult),
                    deps=amax(ev_b, bank_free.get("ctxT")), tick="dve")
                bank_free["bcst"] = t_n
                bank_free[pc.name] = t_n
            ctxT_done = kb.tok("dve")
            hproj_mm_last = None

            for i in range(4):
                xi = xib[:, 0:D]
                hp_toks = []
                for half in range(2):
                    pq = p_q[half]
                    pairs = [(ctxT[:, h2 * 512 + i * 128:
                                   h2 * 512 + i * 128 + 128],
                              wo_s[:, h2 * D + half * 512:
                                   h2 * D + half * 512 + 512])
                             for h2 in range(8)]
                    mm = e_mm_acc(kb, pq[:], pairs,
                                  deps=amax(tok_wo, ctxT_done,
                                            bank_free.get(pq.name)))
                    hproj_mm_last = mm
                    t_x = kb.op("dve", lambda v, xi=xi, half=half, pq=pq:
                                v.tensor_tensor(
                                    xi[:, half * 512:half * 512 + 512], pq[:],
                                    biasb[:, 3 * D + half * 512:
                                          3 * D + half * 512 + 512], ALU.add),
                                deps=amax(mm, bank_free.get("xib")),
                                tick="dve")
                    t_x2 = kb.op("dve", lambda v, xi=xi, half=half, i=i:
                                 v.tensor_tensor(
                                     xi[:, half * 512:half * 512 + 512],
                                     xi[:, half * 512:half * 512 + 512],
                                     tb[:, i * D + half * 512:
                                        i * D + half * 512 + 512], ALU.add),
                                 deps=(t_x,), tick="dve")
                    bank_free[pq.name] = t_x2
                    hp_toks.append(t_x2)
                    tb_last_read[i] = t_x2
                ln2, _ = e_layernorm(
                    kb, xi, D, xi, regB[:, EXC:EXC + D], stats, 8, eps12,
                    deps=amax(*hp_toks), gb=biasb[:, 4 * D:5 * D],
                    bb=biasb[:, 5 * D:6 * D],
                    war_sq=amax(exc_war.get(0), exc_war.get(1)))
                ln2 = (ln2,)
                t_add = kb.op("dve", lambda v, xi=xi, i=i: v.tensor_tensor(
                    xi, xi, xb[:, i * D:(i + 1) * D], ALU.add),
                    deps=ln2, tick="dve")
                kb.dma("sp", xout[512 * b + 128 * i:512 * b + 128 * (i + 1), :],
                       xi, deps=(t_add,), sem="xstore")
                x_last_read[b] = t_add
                bank_free["xib"] = kb.tok("xstore")
            bank_free["ctxT"] = hproj_mm_last
            if b + 1 < 8:
                xload_emit(b + 1)

        kb.op("sp", lambda e: e.nop(), deps=(kb.tok("xstore"),))

        with ExitStack() as st2:
            kb.alloc_sems(st2)
            with nc.Block() as block:
                kb.finalize(block)
    return kb



def _shard_tokens(x):
    xf = np.ascontiguousarray(np.asarray(x, np.float32).reshape(4, 2, TPC, D))
    return [xf[c // 2, c % 2] for c in range(NCORES)]


_ident128 = np.eye(128, dtype=ml_dtypes.bfloat16)


def _rep(v):
    v = np.asarray(v, np.float32).reshape(1, -1)
    return np.broadcast_to(v, (128, v.size))


def run_block1(inputs, xcur):
    nc = bass.Bass("TRN2", target_bir_lowering=False)
    build_block1(nc)
    biasr = np.ascontiguousarray(np.concatenate(
        [_rep(inputs["L0_attn_a_" + k]) for k in
         ("bq", "bk", "bv", "bo", "ln_g", "ln_b")], axis=1))
    common = {
        "sembT": np.ascontiguousarray(
            np.asarray(inputs["L0_attn_a_source_embedding"], np.float32).T),
        "vembT": np.ascontiguousarray(
            np.asarray(inputs["L0_attn_a_value_embedding"], np.float32).T),
        "Wq": np.ascontiguousarray(inputs["L0_attn_a_Wq"], dtype=np.float32),
        "Wk": np.ascontiguousarray(inputs["L0_attn_a_Wk"], dtype=np.float32),
        "Wv": np.ascontiguousarray(inputs["L0_attn_a_Wv"], dtype=np.float32),
        "Wo": np.ascontiguousarray(inputs["L0_attn_a_Wo"], dtype=np.float32),
        "biasr": biasr, "ident": _ident128}
    shards = _shard_tokens(xcur)
    in_maps = [dict(common, xc=np.ascontiguousarray(shards[c]))
               for c in range(NCORES)]
    res = run_bass_kernel_spmd(nc, in_maps, core_ids=list(range(NCORES)))
    out = np.empty((4, 2, TPC, D), dtype=np.float32)
    for c in range(NCORES):
        out[c // 2, c % 2] = res.results[c]["xout"]
    return out.reshape(4, SEQ, D)


def kernel(**inputs):
    x = np.asarray(inputs["L0_attn_a_target_embedding"], dtype=np.float32)
    x = run_block1(inputs, x)
    x = run_l0ffn(inputs, x)
    x = run_block3(inputs, x)
    x = run_l0ffn(inputs, x)
    x = run_block5(inputs, x)
    x = run_block6(inputs, x)
    x = run_block7(inputs, x)
    x = run_block6(inputs, x)
    return x



def build_block5(nc):
    kb = KB(nc)
    dt = nc.dram_tensor
    xfull = dt("xfull", [SEQ, D], F32, kind="ExternalInput").ap()
    hs_flag = 0
    own0 = dt("own0", [1], mybir.dt.int32, kind="ExternalInput").ap()
    WqT = dt("WqT", [D, D], F32, kind="ExternalInput").ap()
    WkT = dt("WkT", [D, D], F32, kind="ExternalInput").ap()
    WvT = dt("WvT", [D, D], F32, kind="ExternalInput").ap()
    WoT = dt("WoT", [D, D], F32, kind="ExternalInput").ap()
    biasr5 = dt("biasr5", [128, 4 * D], F32, kind="ExternalInput").ap()
    stdp_rep = dt("stdp_rep", [128, 8], F32, kind="ExternalInput").ap()
    ident_d = dt("ident", [128, 128], BF16, kind="ExternalInput").ap()
    xown = dt("xown", [TPC, D], F32, kind="ExternalInput").ap()
    xout = dt("xout", [TPC, D], F32, kind="ExternalOutput").ap()

    st = ExitStack()
    with st:
        def sb(name, shape, dtype):
            return st.enter_context(nc.sbuf_tensor(name, shape, dtype))

        def psum(name, shape, dtype=F32):
            return st.enter_context(nc.psum_tensor(name, shape, dtype))

        wq_s = sb("wq", [128, 8 * D], BF16)
        wk_s = sb("wk", [128, 8 * D], BF16)
        wv_s = sb("wv", [128, 8 * D], BF16)
        wo_s = sb("wo", [128, 8 * D], BF16)
        biasb = sb("biasb", [128, 4 * D], F32)
        sigt = sb("sigt", [128, 8], F32)
        ident = sb("ident_s", [128, 128], BF16)
        kva = sb("kva", [128, 8 * 130], BF16)
        xb = sb("xb", [128, 4 * D], F32)
        tb = sb("tb", [128, 4 * D], F32)
        tbf = sb("tbf", [128, 2 * D], BF16)
        tT = sb("tT", [128, 8 * 512], BF16)
        kbuf = sb("kbuf", [128, 2 * D], BF16)
        vbuf5 = sb("vbuf5", [128, 2 * D], BF16)
        onesb = sb("onesb", [128, 1], BF16)
        qbuf = sb("qbuf", [128, 2 * D], BF16)
        qT = sb("qT", [128, 8 * 512], BF16)
        attnT = sb("attnT", [128, 8 * 512], BF16)
        xib = sb("xib", [128, D], F32)
        scr = sb("scr", [128, 2 * D], F32)
        bcst = sb("bcst", [128, 512], F32)
        stats = sb("stats", [128, 64], F32)
        rden = sb("rden", [1, 512], F32)
        onesf = sb("onesf", [1, 128], F32)
        konst = sb("konst", [128, 2], F32)

        p_t = [psum("p_t0", [128, 512], BF16), psum("p_t1", [128, 512], BF16)]
        p_q = [psum("p_q0", [128, 512]), psum("p_q1", [128, 512])]
        p_kv = [psum("p_kv0", [128, 512]), psum("p_kv1", [128, 512])]
        p_ks = psum("p_ks", [128, 8])

        bank_free = {}

        def wload(dst2d, src, deps=()):
            sap = src.rearrange("(c p) d -> p c d", p=128)
            return kb.dma("pool", grouped(dst2d, 8), sap, deps=deps,
                          sem="wload")

        wload(wq_s[:], WqT)
        wload(wk_s[:], WkT)
        wload(wv_s[:], WvT)
        tok_w4 = wload(wo_s[:], WoT)
        tok_wq = tok_wk = tok_wv = tok_wo = tok_w4
        kb.dma("sp", biasb[:], biasr5, sem="cload")
        kb.dma("sp", ident[:], ident_d, sem="cload")
        tok_sdp = kb.dma("sp", sigt[:], stdp_rep, sem="cload")
        tok_bias = tok_id = tok_sdp
        tok_ones = kb.op("dve", lambda v: v.memset(onesf[:], 1.0), tick="dve")
        tok_onesb = kb.op("dve", lambda v: v.memset(onesb[:], 1.0), tick="dve")
        kb.op("dve", lambda v: v.memset(konst[:, 0:1], 1e-5))
        tok_k = kb.op("dve", lambda v: v.memset(konst[:, 1:2], 1e-6),
                      tick="dve")
        eps5 = (konst[:, 0:1], tok_k)

        t_sig = kb.op("act", lambda s: s.activation(
            sigt[:], sigt[:], AF.Sigmoid), deps=(tok_sdp,), tick="act")
        tok_wo2 = kb.op("dve", lambda v: v.tensor_tensor(
            grouped(wo_s[:], 8), grouped(wo_s[:], 8),
            grouped(sigt[:], 8).broadcast_to([128, 8, D]), ALU.mult),
            deps=amax(t_sig, tok_wo), tick="dve")

        def bias3(r, half):
            return grouped(biasb[:, r * D + half * 512:
                                 r * D + half * 512 + 512], 8)

        def elup1(dst, psrc, b3, mm_tok, par, dst_war=None):
            a = scr[:, 1024 * par: 1024 * par + 512]
            b = scr[:, 1024 * par + 512: 1024 * par + 1024]
            t1 = kb.op("dve", lambda v: v.tensor_tensor(
                grouped(a, 8), grouped(psrc[:], 8), b3, ALU.add),
                deps=amax(mm_tok, tok_bias, dst_war), tick="dve")
            t2 = kb.op("dve", lambda v: v.tensor_scalar_min(b, a, 0.0),
                       deps=(t1,), tick="dve")
            t3 = kb.op("act", lambda s: s.activation(b, b, AF.Exp),
                       deps=(t2,), tick="act")
            t4 = kb.op("dve", lambda v: v.tensor_scalar_max(a, a, 0.0),
                       deps=(t1,), tick="dve")
            t5 = kb.op("dve", lambda v: v.tensor_tensor(
                dst, a, b, ALU.add), deps=amax(t3, t4), tick="dve")
            return t5, t1

        NT_A = SEQ // 128
        kv_mm_last = None
        xloadA = [None] * 16
        xA_last_read = [None] * 16

        def xloadA_emit(blk):
            src = xfull[512 * blk:512 * (blk + 1), :].rearrange(
                "(a p) d -> p a d", p=128)
            xloadA[blk] = kb.dma(
                "sp", grouped(xb[:], 4), src,
                deps=amax(xA_last_read[blk - 1] if blk >= 1 else None),
                sem="xload")

        tbfc_war = {}
        kbuf_war = {}
        vk_war = {}
        xloadA_emit(0)
        for blk in range(16):
            xl = xloadA[blk]
            for i in range(4):
                par = i % 2
                x_ap = xb[:, i * D:(i + 1) * D]
                t_f = tb[:, i * D:(i + 1) * D]
                tbfc = tbf[:, par * D:par * D + D]
                ln_tok, bf_tok = e_layernorm(
                    kb, x_ap, D, t_f, tbfc, stats, 4 * par, eps5,
                    deps=(xl,), out_bf=tbfc, war_sq=tbfc_war.get(par))
                evs = []
                for g in range(2):
                    pt = p_t[g]
                    srcs = [tbfc[:, 128 * (4 * g + c):128 * (4 * g + c + 1)]
                            for c in range(4)]
                    base = (4 * g) * 512 + i * 128
                    dst = bass.AP(tT, base, [[8 * 512, 128], [512, 4], [1, 128]])
                    ev = e_transpose4(kb, srcs, ident[:], pt, dst,
                                      deps=(bf_tok, tok_id),
                                      bank_free=bank_free.get(pt.name))
                    bank_free[pt.name] = ev
                    evs.append(ev)
                tbfc_war[par] = evs[-1]
                k_toks = []
                for half in range(2):
                    pq = p_q[half]
                    pairs = [(tT[:, d2 * 512 + i * 128:d2 * 512 + i * 128 + 128],
                              wk_s[:, d2 * D + half * 512:
                                   d2 * D + half * 512 + 512])
                             for d2 in range(8)]
                    mm = e_mm_acc(kb, pq[:], pairs,
                                  deps=amax(tok_wk, evs, bank_free.get(pq.name)))
                    dst = kbuf[:, par * D + half * 512:par * D + half * 512 + 512]
                    tk, pfree = elup1(dst, pq, bias3(1, half), mm, half,
                                      dst_war=kbuf_war.get((par, half)))
                    bank_free[pq.name] = pfree
                    k_toks.append(tk)
                v_toks = []
                for half in range(2):
                    pq = p_q[half]
                    pairs = [(tT[:, d2 * 512 + i * 128:d2 * 512 + i * 128 + 128],
                              wv_s[:, d2 * D + half * 512:
                                   d2 * D + half * 512 + 512])
                             for d2 in range(8)]
                    mm = e_mm_acc(kb, pq[:], pairs,
                                  deps=amax(tok_wv, evs, bank_free.get(pq.name)))
                    dst = vbuf5[:, par * D + half * 512:
                                par * D + half * 512 + 512]
                    tv = kb.op("dve", lambda v, dst=dst, pq=pq, half=half:
                               v.tensor_tensor(dst, pq[:],
                                               biasb[:, 2 * D + half * 512:
                                                     2 * D + half * 512 + 512],
                                               ALU.add),
                               deps=amax(mm, tok_bias, vk_war.get((par, half))),
                               tick="dve")
                    bank_free[pq.name] = tv
                    v_toks.append(tv)
                last = None
                for pr in range(8):
                    pk = p_kv[pr // 4]
                    lhs = kbuf[:, par * D + pr * 128:par * D + (pr + 1) * 128]
                    rhs = vbuf5[:, par * D + pr * 128:par * D + (pr + 1) * 128]
                    first = (blk == 0 and i == 0 and pr % 4 == 0)
                    kb.op("pe", lambda t, pk=pk, lhs=lhs, rhs=rhs, pr=pr,
                          first=first: t.matmul(
                              pk[:, (pr % 4) * 128:(pr % 4 + 1) * 128], lhs,
                              rhs, start=first, stop=False,
                              skip_group_check=True),
                          deps=amax(*k_toks) + amax(*v_toks))
                    last = kb.op(
                        "pe", lambda t, lhs=lhs, pr=pr, blk=blk, i=i:
                        t.matmul(p_ks[:, pr:pr + 1], lhs, onesb[:],
                                 start=(blk == 0 and i == 0 and pr == 0),
                                 stop=False, skip_group_check=True),
                        deps=amax(tok_onesb),
                        tick="pe" if pr == 7 else None)
                for half in range(2):
                    kbuf_war[(par, half)] = last
                    vk_war[(par, half)] = last
                kv_mm_last = last
            xA_last_read[blk] = kv_mm_last
            if blk + 1 < 16:
                xloadA_emit(blk + 1)

        ev_kv = []
        for pr in range(8):
            pk = p_kv[pr // 4]
            col = (pr % 4) * 128
            base = pr * 130
            e1 = kb.op("dve", lambda v, pk=pk, col=col, base=base:
                       v.tensor_copy(kva[0:64, base:base + 64],
                                     pk[0:64, col:col + 64]),
                       deps=(kv_mm_last,), tick="dve")
            e2 = kb.op("dve", lambda v, pk=pk, col=col, base=base:
                       v.tensor_copy(kva[64:128, base + 65:base + 129],
                                     pk[64:128, col + 64:col + 128]),
                       deps=(kv_mm_last,), tick="dve")
            e3 = kb.op("dve", lambda v, pr=pr, base=base:
                       v.tensor_copy(kva[0:64, base + 64:base + 65],
                                     p_ks[0:64, pr:pr + 1]),
                       deps=(kv_mm_last,), tick="dve")
            e4 = kb.op("dve", lambda v, pr=pr, base=base:
                       v.tensor_copy(kva[64:128, base + 129:base + 130],
                                     p_ks[64:128, pr:pr + 1]),
                       deps=(kv_mm_last,), tick="dve")
            ev_kv += [e1, e2, e3, e4]
        tok_kva = amax(*ev_kv)

        xload = [None] * 8
        x_last_read = [None] * 8
        qbuf_war = {}
        qT_war = {}

        def xload_emit(b):
            src = xown[512 * b:512 * (b + 1), :].rearrange(
                "(a p) d -> p a d", p=128)
            xload[b] = kb.dma("sp", grouped(xb[:], 4), src,
                              deps=amax(x_last_read[b - 1] if b >= 1 else
                                        (kv_mm_last if b == 0 else None)),
                              sem="xload")

        x_last_read_A = kv_mm_last
        xload_emit(0)
        for b in range(8):
            xl = xload[b]
            qT_ev = []
            for i in range(4):
                par = i % 2
                x_ap = xb[:, i * D:(i + 1) * D]
                t_f = tb[:, i * D:(i + 1) * D]
                tbfc = tbf[:, par * D:par * D + D]
                ln_tok, bf_tok = e_layernorm(
                    kb, x_ap, D, t_f, tbfc, stats, 4 * par, eps5,
                    deps=(xl,), out_bf=tbfc, war_sq=tbfc_war.get(par))
                evs = []
                for g in range(2):
                    pt = p_t[g]
                    srcs = [tbfc[:, 128 * (4 * g + c):128 * (4 * g + c + 1)]
                            for c in range(4)]
                    base = (4 * g) * 512 + i * 128
                    dst = bass.AP(tT, base, [[8 * 512, 128], [512, 4], [1, 128]])
                    ev = e_transpose4(kb, srcs, ident[:], pt, dst,
                                      deps=(bf_tok, tok_id),
                                      bank_free=bank_free.get(pt.name))
                    bank_free[pt.name] = ev
                    evs.append(ev)
                tbfc_war[par] = evs[-1]
                for half in range(2):
                    pq = p_q[half]
                    pairs = [(tT[:, d2 * 512 + i * 128:d2 * 512 + i * 128 + 128],
                              wq_s[:, d2 * D + half * 512:
                                   d2 * D + half * 512 + 512])
                             for d2 in range(8)]
                    mm = e_mm_acc(kb, pq[:], pairs,
                                  deps=amax(tok_wq, evs, bank_free.get(pq.name)))
                    dst = qbuf[:, par * D + half * 512:par * D + half * 512 + 512]
                    tq, pfree = elup1(dst, pq, bias3(0, half), mm, half,
                                      dst_war=qbuf_war.get((par, half)))
                    bank_free[pq.name] = pfree
                    qbuf_war[(par, half)] = tq
                    pt = p_t[half]
                    srcs = [qbuf[:, par * D + half * 512 + 128 * c:
                                 par * D + half * 512 + 128 * (c + 1)]
                            for c in range(4)]
                    base = (half * 4) * 512 + i * 128
                    dst = bass.AP(qT, base, [[8 * 512, 128], [512, 4], [1, 128]])
                    ev = e_transpose4(kb, srcs, ident[:], pt, dst,
                                      deps=(tq, tok_id),
                                      bank_free=bank_free.get(pt.name),
                                      evac_war=qT_war.get((par, half)))
                    bank_free[pt.name] = ev
                    qbuf_war[(par, half)] = amax(qbuf_war[(par, half)], ev)
                    qT_war[(par, half)] = ev
                    qT_ev.append(ev)
            for h in range(NHEADS):
                h2, hh = h // 2, h % 2
                pn = p_kv[h % 2]
                lhs = kva[hh * 64:hh * 64 + 64, h2 * 130 + hh * 65:
                          h2 * 130 + hh * 65 + 65]
                rhs = qT[hh * 64:hh * 64 + 64, h2 * 512:h2 * 512 + 512]
                mm_n = kb.op("pe", lambda t, pn=pn, lhs=lhs, rhs=rhs:
                             t.matmul(pn[0:65, :], lhs, rhs, start=True,
                                      stop=True),
                             deps=amax(*qT_ev) + amax(tok_kva,
                                                      bank_free.get(pn.name)),
                             tick="pe")
                t_eps = kb.op("dve", lambda v, pn=pn: v.tensor_scalar(
                    rden[0:1, :], pn[HE:HE + 1, :], 1e-6, None, ALU.add),
                    deps=amax(mm_n, bank_free.get("rden")), tick="dve")
                t_r = kb.op("dve", lambda v: v.reciprocal(
                    rden[0:1, :], rden[0:1, :]), deps=(t_eps,), tick="dve")
                psb = p_t[h % 2]
                psb = p_q[h % 2]
                mm_b = kb.op("pe", lambda t, psb=psb: t.matmul(
                    psb[0:HE, :], onesf[0:1, 0:HE], rden[0:1, :],
                    start=True, stop=True),
                    deps=amax(t_r, tok_ones, bank_free.get(psb.name)),
                    tick="pe")
                bank_free["rden"] = mm_b
                ev_b = kb.op("act", lambda s, psb=psb: s.copy(
                    bcst[0:HE, :], psb[0:HE, :]),
                    deps=amax(mm_b, bank_free.get("bcst")), tick="act")
                bank_free[psb.name] = ev_b
                dst = attnT[hh * 64:hh * 64 + 64, h2 * 512:h2 * 512 + 512]
                t_n = kb.op("dve", lambda v, dst=dst, pn=pn: v.tensor_tensor(
                    dst, pn[0:HE, :], bcst[0:HE, :], ALU.mult),
                    deps=amax(ev_b, bank_free.get("attnT")), tick="dve")
                bank_free["bcst"] = t_n
                bank_free[pn.name] = t_n
            attnT_done = kb.tok("dve")
            hproj_mm_last = None
            for i in range(4):
                xi = xib[:, 0:D]
                hp_toks = []
                for half in range(2):
                    pq = p_q[half]
                    pairs = [(attnT[:, h2 * 512 + i * 128:
                                    h2 * 512 + i * 128 + 128],
                              wo_s[:, h2 * D + half * 512:
                                   h2 * D + half * 512 + 512])
                             for h2 in range(8)]
                    mm = e_mm_acc(kb, pq[:], pairs,
                                  deps=amax(tok_wo2, attnT_done,
                                            bank_free.get(pq.name)))
                    hproj_mm_last = mm
                    t_x = kb.op("dve", lambda v, xi=xi, half=half, pq=pq:
                                v.tensor_tensor(
                                    xi[:, half * 512:half * 512 + 512], pq[:],
                                    biasb[:, 3 * D + half * 512:
                                          3 * D + half * 512 + 512], ALU.add),
                                deps=amax(mm, bank_free.get("xib"), tok_bias),
                                tick="dve")
                    t_x2 = kb.op("dve", lambda v, xi=xi, half=half, i=i:
                                 v.tensor_tensor(
                                     xi[:, half * 512:half * 512 + 512],
                                     xi[:, half * 512:half * 512 + 512],
                                     xb[:, i * D + half * 512:
                                        i * D + half * 512 + 512], ALU.add),
                                 deps=(t_x,), tick="dve")
                    bank_free[pq.name] = t_x2
                    hp_toks.append(t_x2)
                t_last = hp_toks[-1]
                kb.dma("sp", xout[512 * b + 128 * i:512 * b + 128 * (i + 1), :],
                       xi, deps=amax(*hp_toks), sem="xstore")
                x_last_read[b] = amax(*hp_toks)
                bank_free["xib"] = kb.tok("xstore")
            bank_free["attnT"] = hproj_mm_last
            if b + 1 < 8:
                xload_emit(b + 1)

        kb.op("sp", lambda e: e.nop(), deps=(kb.tok("xstore"),))

        with ExitStack() as st2:
            kb.alloc_sems(st2)
            with nc.Block() as block:
                kb.finalize(block)
    return kb


def run_block5(inputs, xcur):
    nc = bass.Bass("TRN2", target_bir_lowering=False)
    build_block5(nc)
    biasr5 = np.ascontiguousarray(np.concatenate(
        [_rep(inputs["L1_attn_a_" + k]) for k in ("bq", "bk", "bv", "bo")],
        axis=1))
    stdp = np.asarray(inputs["L1_attn_a_stdp"], np.float32)
    stdp_rep = np.empty((128, 8), np.float32)
    for c in range(8):
        for p in range(128):
            stdp_rep[p, c] = stdp[2 * c + p // 64]
    common = {
        "WqT": np.ascontiguousarray(np.asarray(inputs["L1_attn_a_Wq"],
                                               np.float32).T),
        "WkT": np.ascontiguousarray(np.asarray(inputs["L1_attn_a_Wk"],
                                               np.float32).T),
        "WvT": np.ascontiguousarray(np.asarray(inputs["L1_attn_a_Wv"],
                                               np.float32).T),
        "WoT": np.ascontiguousarray(np.asarray(inputs["L1_attn_a_Wo"],
                                               np.float32).T),
        "biasr5": biasr5, "stdp_rep": stdp_rep, "ident": _ident128,
        "own0": np.zeros(1, np.int32)}
    x4 = np.asarray(xcur, np.float32)
    in_maps = []
    for c in range(NCORES):
        bat, hf = c // 2, c % 2
        in_maps.append(dict(
            common,
            xfull=np.ascontiguousarray(x4[bat]),
            xown=np.ascontiguousarray(x4[bat, hf * TPC:(hf + 1) * TPC])))
    res = run_bass_kernel_spmd(nc, in_maps, core_ids=list(range(NCORES)))
    out = np.empty((4, 2, TPC, D), dtype=np.float32)
    for c in range(NCORES):
        out[c // 2, c % 2] = res.results[c]["xout"]
    return out.reshape(4, SEQ, D)



def build_block6(nc, nexp=8, nhalf=2):
    kb = KB(nc)
    dt = nc.dram_tensor
    xc = dt("xc", [TPC, D], F32, kind="ExternalInput").ap()
    W1 = dt("W1", [8, D, 4 * D], F32, kind="ExternalInput").ap()
    W2 = dt("W2", [8, 4 * D, D], F32, kind="ExternalInput").ap()
    rwT = dt("rwT", [D, 8], F32, kind="ExternalInput").ap()
    gbeta = dt("gbeta", [128, 2 * D], F32, kind="ExternalInput").ap()
    identf_d = dt("identf", [128, 128], F32, kind="ExternalInput").ap()
    xout = dt("xout", [TPC, D], F32, kind="ExternalOutput").ap()
    w2b_d = dt("w2b_d", [8, 4 * D, D], BF16, kind="Internal").ap()
    y_d = dt("y_d", [TPC, D], F32, kind="Internal").ap()

    st = ExitStack()
    with st:
        def sb(name, shape, dtype):
            return st.enter_context(nc.sbuf_tensor(name, shape, dtype))

        def psum(name, shape, dtype=F32):
            return st.enter_context(nc.psum_tensor(name, shape, dtype))

        acc = sb("acc", [128, 16 * D], F32)
        ynT = sb("ynT", [128, 8 * 2048], BF16)
        hT = sb("hT", [128, 16 * 512], BF16)
        w1sb = sb("w1sb", [128, 8 * 2048], BF16)
        w2sb = sb("w2sb", [128, 2 * 1024], BF16)
        rw_s = sb("rw_s", [128, 64], BF16)
        gbb = sb("gbb", [128, 2 * D], F32)
        identf = sb("identf_s", [128, 128], F32)
        xb = sb("xb", [128, 4 * D], F32)
        yb = sb("yb", [128, D], F32)
        ynb = sb("ynb", [128, D], F32)
        wdbuf = sb("wdbuf", [128, 16 * 8], F32)
        p8 = sb("p8", [128, 4 * 8], F32)
        stats = sb("stats", [128, 64], F32)
        konst = sb("konst", [128, 2], F32)

        pb = [psum("pb%d" % i, [128, 512]) for i in range(8)]
        bank_free = {}

        kb.dma("sp", gbb[:], gbeta, sem="cload")
        tok_idf = kb.dma("sp", identf[:], identf_d, sem="cload")
        tok_rw = kb.dma("pool", grouped(rw_s[:], 8),
                        rwT.rearrange("(c p) e -> p c e", p=128), sem="crw")
        tok_gb = tok_idf
        kb.op("dve", lambda v: v.memset(konst[:, 0:1], 1e-5))
        tok_k = kb.op("dve", lambda v: v.memset(konst[:, 1:2], 1e-5),
                      tick="dve")
        eps5 = (konst[:, 0:1], tok_k)

        tok_w2c = []
        for e in range(8):
            tok_w2c.append(kb.dma("pool", w2b_d[e], W2[e], sem="wcast"))
        tok_wcast_all = kb.tok("wcast")
        tok_w2c = [tok_wcast_all] * 8

        w1_ld = {}

        def load_w1(e, ps_, dep):
            w1_ld[(e, ps_)] = kb.dma(
                "pool", grouped(w1sb[:], 8),
                W1[e][:, ps_ * 2048:(ps_ + 1) * 2048].rearrange(
                    "(c p) f -> p c f", p=128),
                deps=amax(dep), sem="w1ld")

        def halfpass(hv):
            base_t = hv * 2048
            tr_war = None
            for blk in range(4):
                gt = hv * 16 + blk * 4
                src = xc[base_t + 512 * blk: base_t + 512 * (blk + 1), :]
                xl = kb.dma("sp", grouped(xb[:], 4),
                            src.rearrange("(a p) d -> p a d", p=128),
                            deps=amax(bank_free.get("xb")), sem="xload")
                for i in range(4):
                    par = i % 2
                    ti = blk * 4 + i
                    x_ap = xb[:, i * D:(i + 1) * D]
                    y_f = yb[:, 0:D]
                    yn_f = ynb[:, 0:D]
                    ln1, _ = e_layernorm(
                        kb, x_ap, D, y_f, hT[:, 0:D], stats, 4 * par, eps5,
                        deps=(xl,), war_out=amax(bank_free.get("yb"),
                                                 bank_free.get("ybst")),
                        war_sq=bank_free.get("hT0"))
                    ys = kb.dma("sp", y_d[base_t + 128 * ti:
                                          base_t + 128 * (ti + 1), :], y_f,
                                deps=(ln1,), sem="yst")
                    bank_free["ybst"] = ys
                    ln2, _ = e_layernorm(
                        kb, y_f, D, yn_f, hT[:, 0:D], stats, 8 + 4 * par, eps5,
                        deps=(ln1,), gb=gbb[:, 0:D], bb=gbb[:, D:2 * D],
                        war_out=bank_free.get("ynb"),
                        war_sq=bank_free.get("hT0"))
                    bank_free["yb"] = ln2
                    evs = []
                    for g in range(2):
                        pt = pb[6 + g]
                        t = None
                        for c in range(4):
                            s_ap = yn_f[:, 128 * (4 * g + c):
                                        128 * (4 * g + c + 1)]
                            t = kb.op("pe", lambda te, s_ap=s_ap, c=c, pt=pt:
                                      te.transpose(pt[:, 128 * c:128 * (c + 1)],
                                                   s_ap, identf[:]),
                                      deps=amax(ln2, tok_idf,
                                                bank_free.get(pt.name))
                                      if c == 0 else (),
                                      tick="pe" if c == 3 else None)
                        dst = bass.AP(ynT, (4 * g) * 2048 + ti * 128,
                                      [[8 * 2048, 128], [2048, 4], [1, 128]])
                        ev = kb.op("act", lambda s, dst=dst, pt=pt:
                                   s.copy(dst, grouped(pt[:], 4)),
                                   deps=(t,), tick="act")
                        bank_free[pt.name] = ev
                        evs.append(ev)
                    bank_free["ynb"] = evs[-1]
                    pr = pb[5]
                    mm = e_mm_acc(
                        kb, pr[:, 0:8],
                        [(bass.AP(ynT, d2 * 2048 + ti * 128,
                                  [[8 * 2048, 128], [1, 128]]),
                          rw_s[:, d2 * 8:(d2 + 1) * 8]) for d2 in range(8)],
                        deps=amax(tok_rw, evs, bank_free.get(pr.name)))
                    pcol = p8[:, par * 16:par * 16 + 8]
                    qcol = p8[:, par * 16 + 8:par * 16 + 16]
                    sm = stats[:, 16 + 8 * par:16 + 8 * par + 4]
                    t1 = kb.op("dve", lambda v, pr=pr, sm=sm: v.reduce_max(
                        sm[:, 0:1], pr[:, 0:8], axis=AX.X),
                        deps=amax(mm, bank_free.get(("p8", par))), tick="dve")
                    t2 = kb.op("dve", lambda v, sm=sm: v.tensor_scalar_mul(
                        sm[:, 0:1], sm[:, 0:1], -1.0 / 1.5), deps=(t1,),
                        tick="dve")
                    t3 = kb.op("act", lambda s, pr=pr, pcol=pcol, sm=sm:
                               s.activation(pcol, pr[:, 0:8], AF.Exp,
                                            scale=1.0 / 1.5, bias=sm[:, 0:1],
                                            accum_out=sm[:, 1:2]),
                               deps=(t2,), tick="act")
                    bank_free[pr.name] = t3
                    t4 = kb.op("dve", lambda v, sm=sm: v.reciprocal(
                        sm[:, 2:3], sm[:, 1:2]), deps=(t3,), tick="dve")
                    t5 = kb.op("dve", lambda v, pcol=pcol, sm=sm:
                               v.tensor_scalar(pcol, pcol, sm[:, 2:3], None,
                                               ALU.mult), deps=(t4,),
                               tick="dve")
                    t6 = kb.op("dve", lambda v, pcol=pcol, sm=sm: v.reduce_max(
                        sm[:, 0:1], pcol, axis=AX.X), deps=(t5,), tick="dve")
                    t7 = kb.op("dve", lambda v, pcol=pcol, qcol=qcol, sm=sm:
                               v.tensor_scalar(qcol, pcol, sm[:, 0:1], None,
                                               ALU.is_lt), deps=(t6,),
                               tick="dve")
                    t8 = kb.op("dve", lambda v, pcol=pcol, qcol=qcol:
                               v.tensor_tensor(qcol, qcol, pcol, ALU.mult),
                               deps=(t7,), tick="dve")
                    t9 = kb.op("dve", lambda v, qcol=qcol, sm=sm: v.reduce_max(
                        sm[:, 1:2], qcol, axis=AX.X), deps=(t8,), tick="dve")
                    t10 = kb.op("dve", lambda v, pcol=pcol, qcol=qcol, sm=sm:
                                v.tensor_scalar(qcol, pcol, sm[:, 1:2], None,
                                                ALU.is_ge), deps=(t9,),
                                tick="dve")
                    wcol = wdbuf[:, ti * 8:(ti + 1) * 8]
                    t11 = kb.op("dve", lambda v, pcol=pcol, qcol=qcol,
                                wcol=wcol: v.tensor_tensor(
                                    wcol, pcol, qcol, ALU.mult),
                                deps=amax(t10, bank_free.get("wd")),
                                tick="dve")
                    bank_free[("p8", par)] = t11
                    bank_free[("wd", ti)] = t11
                bank_free["xb"] = kb.tok("dve")
            wd_done = kb.tok("dve")
            ynT_done = kb.tok("act")
            bank_free["hT0"] = kb.tok("act")

            for e in range(nexp):
                if hv == 0 and e == 0:
                    load_w1(0, 0, ynT_done)
                for ps_ in range(2):
                    fbase = ps_ * 16
                    w1_last_use = None
                    for blk in range(4):
                        ti0 = blk * 4
                        for fi in range(16):
                            hp = pb[6 + (fi % 2)]
                            pairs = [(w1sb[:, c * 2048 + fi * 128:
                                           c * 2048 + fi * 128 + 128],
                                      ynT[:, c * 2048 + blk * 512:
                                          c * 2048 + blk * 512 + 512])
                                     for c in range(8)]
                            mm = e_mm_acc(kb, hp[:], pairs,
                                          deps=amax(w1_ld[(e, ps_)], ynT_done,
                                                    bank_free.get(hp.name)))
                            w1_last_use = mm
                            sl = kb.op("act", lambda s, fi=fi, hp=hp:
                                       s.activation(
                                           hT[:, fi * 512:(fi + 1) * 512],
                                           hp[:], AF.Silu),
                                       deps=amax(mm,
                                                 bank_free.get(("hTc", fi))),
                                       tick="act")
                            bank_free[hp.name] = sl
                            bank_free[("hTs", fi)] = sl
                        wmm = {}
                        for fi in range(16):
                            fc = fbase + fi
                            wb = w2sb[:, (fi % 2) * 1024:(fi % 2) * 1024 + 1024]
                            wl = kb.dma("sp", wb,
                                        w2b_d[e, 128 * fc:128 * (fc + 1), :],
                                        deps=amax(tok_w2c[e],
                                                  bank_free.get(("w2",
                                                                 fi % 2))),
                                        sem="w2l%d" % (fi % 2))
                            for tt in range(4):
                                for hf in range(2):
                                    pp = pb[tt * 2 + hf]
                                    lhs = hT[:, fi * 512 + tt * 128:
                                             fi * 512 + tt * 128 + 128]
                                    rhs = wb[:, hf * 512:hf * 512 + 512]
                                    dep0 = amax(wl, bank_free.get(("hTs", fi)),
                                                bank_free.get(pp.name)
                                                if fi == 0 else None)
                                    wmm[(tt, hf)] = kb.op(
                                        "pe", lambda t, pp=pp, lhs=lhs,
                                        rhs=rhs, fi=fi: t.matmul(
                                            pp[:], lhs, rhs,
                                            start=(fi == 0), stop=(fi == 15),
                                            skip_group_check=True),
                                        deps=dep0,
                                        tick="pe" if (fi == 15 or
                                                      (tt == 3 and hf == 1))
                                        else None)
                            bank_free[("w2", fi % 2)] = wmm[(3, 1)]
                        first_write = (e == 0 and ps_ == 0)
                        for tt in range(4):
                            ti = ti0 + tt
                            wcol = wdbuf[:, ti * 8 + e:ti * 8 + e + 1]
                            for hf in range(2):
                                pp = pb[tt * 2 + hf]
                                a_ap = acc[:, ti * D + hf * 512:
                                           ti * D + hf * 512 + 512]
                                if first_write:
                                    ev = kb.op(
                                        "dve", lambda v, a_ap=a_ap, pp=pp,
                                        wcol=wcol: v.tensor_scalar(
                                            a_ap, pp[:], wcol, None, ALU.mult),
                                        deps=amax(wmm[(tt, hf)], wd_done,
                                                  bank_free.get("accR")),
                                        tick="dve")
                                else:
                                    ev = kb.op(
                                        "dve", lambda v, a_ap=a_ap, pp=pp,
                                        wcol=wcol: v.scalar_tensor_tensor(
                                            a_ap, pp[:], wcol, a_ap,
                                            ALU.mult, ALU.add),
                                        deps=amax(wmm[(tt, hf)], wd_done),
                                        tick="dve")
                                bank_free[pb[tt * 2 + hf].name] = ev
                                bank_free[("acc", ti, hf)] = ev
                        for fi in range(16):
                            bank_free[("hTc", fi)] = wmm[(3, 1)]
                    nxt = None
                    if ps_ == 0:
                        nxt = (e, 1)
                    elif e + 1 < nexp:
                        nxt = (e + 1, 0)
                    elif hv + 1 < nhalf:
                        nxt = (0, 0)
                    if nxt is not None:
                        load_w1(nxt[0], nxt[1], w1_last_use)
            for blk in range(4):
                src = xc[base_t + 512 * blk: base_t + 512 * (blk + 1), :]
                xl = kb.dma("sp", grouped(xb[:], 4),
                            src.rearrange("(a p) d -> p a d", p=128),
                            deps=amax(bank_free.get("xb")), sem="xload")
                outs = []
                for i in range(4):
                    ti = blk * 4 + i
                    buf = yb if i % 2 == 0 else ynb
                    yap = buf[:, 0:D]
                    yl = kb.dma("sp", yap,
                                y_d[base_t + 128 * ti:base_t + 128 * (ti + 1),
                                    :],
                                deps=amax(bank_free.get("ycmb%d" % (i % 2))),
                                sem="yl%d" % (i % 2))
                    a_ap = acc[:, ti * D:(ti + 1) * D]
                    dep = (xl, yl, bank_free.get(("acc", ti, 0)),
                           bank_free.get(("acc", ti, 1)))
                    s1 = kb.op("dve", lambda v, a_ap=a_ap, yap=yap:
                               v.tensor_tensor(a_ap, a_ap, yap, ALU.add),
                               deps=amax(*dep), tick="dve")
                    s2 = kb.op("dve", lambda v, a_ap=a_ap, i=i:
                               v.tensor_tensor(a_ap, a_ap,
                                               xb[:, i * D:(i + 1) * D],
                                               ALU.add), deps=(s1,),
                               tick="dve")
                    kb.dma("sp", xout[base_t + 128 * ti:base_t + 128 * (ti + 1),
                                      :], a_ap, deps=(s2,), sem="xstore")
                    outs.append(s2)
                    bank_free["ycmb%d" % (i % 2)] = s1
                bank_free["xb"] = amax(*outs)
            bank_free["accR"] = kb.tok("xstore")

        for _hv in range(nhalf):
            halfpass(_hv)

        kb.op("sp", lambda e: e.nop(), deps=(kb.tok("xstore"),))

        with ExitStack() as st2:
            kb.alloc_sems(st2)
            with nc.Block() as block:
                kb.finalize(block)
    return kb


_identf128 = np.eye(128, dtype=np.float32)


def run_block6(inputs, xcur):
    nc = bass.Bass("TRN2", target_bir_lowering=False)
    build_block6(nc)
    gbeta = np.ascontiguousarray(np.concatenate(
        [_rep(inputs["L1_ffn_gamma"]), _rep(inputs["L1_ffn_beta"])], axis=1))
    common = {
        "W1": np.ascontiguousarray(inputs["L1_ffn_W1"], dtype=np.float32),
        "W2": np.ascontiguousarray(inputs["L1_ffn_W2"], dtype=np.float32),
        "rwT": np.ascontiguousarray(
            np.asarray(inputs["L1_ffn_router_w"], np.float32).T),
        "gbeta": gbeta, "identf": _identf128}
    shards = _shard_tokens(xcur)
    in_maps = [dict(common, xc=np.ascontiguousarray(shards[c]))
               for c in range(NCORES)]
    res = run_bass_kernel_spmd(nc, in_maps, core_ids=list(range(NCORES)))
    out = np.empty((4, 2, TPC, D), dtype=np.float32)
    for c in range(NCORES):
        out[c // 2, c % 2] = res.results[c]["xout"]
    return out.reshape(4, SEQ, D)



def build_block3(nc, own_start):
    kb = KB(nc)
    dt = nc.dram_tensor
    NKEY = own_start + TPC
    nkb = NKEY // 512
    xfull = dt("xfull", [NKEY, D], F32, kind="ExternalInput").ap()
    xown = dt("xown", [TPC, D], F32, kind="ExternalInput").ap()
    WqT = dt("WqT", [D, D], F32, kind="ExternalInput").ap()
    WkT = dt("WkT", [D, D], F32, kind="ExternalInput").ap()
    WvT = dt("WvT", [D, D], F32, kind="ExternalInput").ap()
    cosT = dt("cosT", [D, SEQ], F32, kind="ExternalInput").ap()
    sinT = dt("sinT", [D, SEQ], F32, kind="ExternalInput").ap()
    perm_d = dt("perm", [128, 128], BF16, kind="ExternalInput").ap()
    trimask_d = dt("trimask", [128, 128], F32, kind="ExternalInput").ap()
    ident_d = dt("ident", [128, 128], BF16, kind="ExternalInput").ap()
    xout = dt("xout", [TPC, D], F32, kind="ExternalOutput").ap()
    kT_d = dt("kT_d", [8, 128, NKEY], BF16, kind="Internal").ap()
    v_d = dt("v_d", [NKEY, D], BF16, kind="Internal").ap()

    st = ExitStack()
    with st:
        def sb(name, shape, dtype):
            return st.enter_context(nc.sbuf_tensor(name, shape, dtype))

        def psum(name, shape, dtype=F32):
            return st.enter_context(nc.psum_tensor(name, shape, dtype))

        wq_s = sb("wq", [128, 8 * D], BF16)
        wk_s = sb("wk", [128, 8 * D], BF16)
        wv_s = sb("wv", [128, 8 * D], BF16)
        perm = sb("perm_s", [128, 128], BF16)
        trimask = sb("trimask_s", [128, 128], F32)
        ident = sb("ident_s", [128, 128], BF16)
        tT = sb("tT", [128, 8 * 512], BF16)
        tbf = sb("tbf", [128, 2 * D], BF16)
        qT = sb("qT", [128, 8 * 512], BF16)
        kTb = sb("kTb", [128, 2 * 8 * 512], BF16)
        vbb = sb("vbb", [128, 2 * 4 * D], BF16)
        ob = sb("ob", [128, 4 * D], F32)
        pbuf = sb("pbuf", [128, 2 * 512], BF16)
        ropet = sb("ropet", [128, 4 * 512], F32)
        rsc = sb("rsc", [128, 2 * 512], F32)
        xb = sb("xb", [128, 4 * D], F32)
        tb = sb("tb", [128, 4 * D], F32)
        stats = sb("stats", [128, 64], F32)
        konst = sb("konst", [128, 2], F32)

        p_t = [psum("p_t0", [128, 512], BF16), psum("p_t1", [128, 512], BF16)]
        p_q = [psum("p_q0", [128, 512]), psum("p_q1", [128, 512])]
        p_s = [psum("p_s0", [128, 512]), psum("p_s1", [128, 512])]
        p_v = [psum("p_v0", [128, 512]), psum("p_v1", [128, 512])]

        bank_free = {}

        def wload(dst2d, src, deps=()):
            sap = src.rearrange("(c p) d -> p c d", p=128)
            return kb.dma("pool", grouped(dst2d, 8), sap, deps=deps,
                          sem="wload")

        wload(wq_s[:], WqT)
        wload(wk_s[:], WkT)
        tok_wv = wload(wv_s[:], WvT)
        tok_wq = tok_wk = tok_wv
        kb.dma("sp", perm[:], perm_d, sem="cload")
        kb.dma("sp", trimask[:], trimask_d, sem="cload")
        tok_id = kb.dma("sp", ident[:], ident_d, sem="cload")
        tok_pm = tok_tm = tok_id
        kb.op("dve", lambda v: v.memset(konst[:, 0:1], 1e-5))
        tok_k = kb.op("dve", lambda v: v.memset(konst[:, 1:2], 0.0),
                      tick="dve")
        eps5 = (konst[:, 0:1], tok_k)

        SCALE = 1.0 / 32.0

        def rope_chunk(dst_bf, src_psum, d2, pos0, ntok, par, deps):
            ct = ropet[:, par * 1024:par * 1024 + 512][:, 0:ntok]
            stt_ = ropet[:, par * 1024 + 512:par * 1024 + 1024][:, 0:ntok]
            lc = kb.dma("sp", ct, cosT[d2 * 128:(d2 + 1) * 128,
                                       pos0:pos0 + ntok],
                        deps=amax(bank_free.get(("rope", par))),
                        sem="rl%d" % par)
            ls = kb.dma("sp", stt_, sinT[d2 * 128:(d2 + 1) * 128,
                                         pos0:pos0 + ntok],
                        deps=amax(bank_free.get(("rope", par))),
                        sem="rl%d" % par)
            stage = rsc[:, par * 512:par * 512 + ntok]
            ev = kb.op("act", lambda s: s.copy(stage, src_psum),
                       deps=amax(deps, bank_free.get(("rst", par))),
                       tick="act")
            ps2 = p_s[par]
            stb = pbuf[:, par * 512:par * 512 + ntok]
            evb = kb.op("dve", lambda v: v.tensor_copy(stb, stage),
                        deps=amax(ev, bank_free.get(("pb", par))), tick="dve")
            mm = kb.op("pe", lambda t: t.matmul(
                ps2[:, 0:ntok], perm[:], stb, start=True, stop=True),
                deps=amax(evb, tok_pm, bank_free.get(ps2.name)), tick="pe")
            t1 = kb.op("dve", lambda v: v.tensor_tensor(
                stage, stage, ct, ALU.mult), deps=amax(lc, ls, ev, mm),
                tick="dve")
            t2 = kb.op("dve", lambda v: v.tensor_tensor(
                stt_, ps2[:, 0:ntok], stt_, ALU.mult),
                deps=amax(ls, mm), tick="dve")
            t3 = kb.op("dve", lambda v: v.tensor_tensor(
                dst_bf, stage, stt_, ALU.add), deps=amax(t1, t2), tick="dve")
            bank_free[("rope", par)] = t3
            bank_free[("rst", par)] = t1
            bank_free[("pb", par)] = t3
            bank_free[ps2.name] = t2
            return t3

        xload = {}
        x_last_read = {}

        def xload_emit(which, b, nblocks, xsrc):
            src = xsrc[512 * b:512 * (b + 1), :].rearrange(
                "(a p) d -> p a d", p=128)
            xload[(which, b)] = kb.dma(
                "sp", grouped(xb[:], 4), src,
                deps=amax(x_last_read.get((which, b - 1))), sem="xload")

        nkblk = NKEY // 512
        tbfc_war = {}
        xload_emit("A", 0, nkblk, xfull)
        for b in range(nkblk):
            xl = xload[("A", b)]
            evs_all = []
            for i in range(4):
                par = i % 2
                x_ap = xb[:, i * D:(i + 1) * D]
                t_f = tb[:, i * D:(i + 1) * D]
                tbfc = tbf[:, par * D:par * D + D]
                ln_tok, bf_tok = e_layernorm(
                    kb, x_ap, D, t_f, tbfc, stats, 4 * par, eps5,
                    deps=(xl,), out_bf=tbfc, war_sq=tbfc_war.get(par))
                evs = []
                for g in range(2):
                    pt = p_t[g]
                    srcs = [tbfc[:, 128 * (4 * g + c):128 * (4 * g + c + 1)]
                            for c in range(4)]
                    dst = bass.AP(tT, (4 * g) * 512 + i * 128,
                                  [[8 * 512, 128], [512, 4], [1, 128]])
                    ev = e_transpose4(kb, srcs, ident[:], pt, dst,
                                      deps=(bf_tok, tok_id),
                                      bank_free=bank_free.get(pt.name))
                    bank_free[pt.name] = ev
                    evs.append(ev)
                tbfc_war[par] = evs[-1]
                evs_all += evs
                for half in range(2):
                    pq = p_q[half]
                    pairs = [(tT[:, d2 * 512 + i * 128:d2 * 512 + i * 128 + 128],
                              wv_s[:, d2 * D + half * 512:
                                   d2 * D + half * 512 + 512])
                             for d2 in range(8)]
                    mm = e_mm_acc(kb, pq[:], pairs,
                                  deps=amax(tok_wv, evs, bank_free.get(pq.name)))
                    stg = vbb[:, half * 512 + (i % 2) * 4 * D:
                              half * 512 + (i % 2) * 4 * D + 512]
                    evv = kb.op("act", lambda s, stg=stg, pq=pq: s.copy(
                        stg, pq[:]), deps=amax(mm,
                                               bank_free.get(("vst", i % 2,
                                                              half))),
                        tick="act")
                    bank_free[pq.name] = evv
                    stv = kb.dma("sp", v_d[512 * b + 128 * i:
                                           512 * b + 128 * (i + 1),
                                           half * 512:half * 512 + 512], stg,
                                 deps=(evv,), sem="kst")
                    bank_free[("vst", i % 2, half)] = stv
            for d2 in range(8):
                pq = p_q[d2 % 2]
                pairs = [(wk_s[:, c * D + d2 * 128: c * D + d2 * 128 + 128],
                          tT[:, c * 512:(c + 1) * 512]) for c in range(8)]
                mm = e_mm_acc(kb, pq[:], pairs,
                              deps=amax(tok_wk, evs_all,
                                        bank_free.get(pq.name)))
                stg = kTb[:, (d2 % 2) * 512:(d2 % 2) * 512 + 512]
                tok3 = rope_chunk(stg, pq[:], d2, 512 * b, 512, d2 % 2,
                                  amax(mm, bank_free.get(("kstg", d2 % 2))))
                bank_free[pq.name] = kb.tok("act")
                stk = kb.dma("sp", kT_d[d2, :, 512 * b:512 * (b + 1)], stg,
                             deps=(tok3,), sem="kst")
                bank_free[("kstg", d2 % 2)] = stk
            x_last_read[("A", b)] = kb.tok("dve")
            if b + 1 < nkblk:
                xload_emit("A", b + 1, nkblk, xfull)
        tok_keys = amax(kb.tok("kst"))

        kld = {}
        vld = {}

        def load_kv(qb, kbi):
            par = kbi % 2
            kld[kbi] = kb.dma(
                "sp", grouped(kTb[:, par * 4096:par * 4096 + 4096], 8),
                kT_d[:, :, 512 * kbi:512 * (kbi + 1)].rearrange(
                    "c p k -> p c k"),
                deps=amax(tok_keys, bank_free.get(("kv", par))),
                sem="kv%d" % par)
            vld[kbi] = kb.dma(
                "sp", grouped(vbb[:, par * 4 * D:(par + 1) * 4 * D], 4),
                v_d[512 * kbi:512 * (kbi + 1), :].rearrange(
                    "(a p) d -> p a d", p=128),
                deps=amax(tok_keys, bank_free.get(("kv", par))),
                sem="kv%d" % par)

        for qb in range(8):
            xload_emit("B", qb, 8, xown)
            xl = xload[("B", qb)]
            if qb == 0:
                kb.wait("sp", x_last_read[("A", nkblk - 1)])
            evs_all = []
            for i in range(4):
                par = i % 2
                x_ap = xb[:, i * D:(i + 1) * D]
                t_f = tb[:, i * D:(i + 1) * D]
                tbfc = tbf[:, par * D:par * D + D]
                ln_tok, bf_tok = e_layernorm(
                    kb, x_ap, D, t_f, tbfc, stats, 4 * par, eps5,
                    deps=(xl,), out_bf=tbfc, war_sq=tbfc_war.get(par))
                evs = []
                for g in range(2):
                    pt = p_t[g]
                    srcs = [tbfc[:, 128 * (4 * g + c):128 * (4 * g + c + 1)]
                            for c in range(4)]
                    dst = bass.AP(tT, (4 * g) * 512 + i * 128,
                                  [[8 * 512, 128], [512, 4], [1, 128]])
                    ev = e_transpose4(kb, srcs, ident[:], pt, dst,
                                      deps=(bf_tok, tok_id),
                                      bank_free=bank_free.get(pt.name))
                    bank_free[pt.name] = ev
                    evs.append(ev)
                tbfc_war[par] = evs[-1]
                evs_all += evs
            for d2 in range(8):
                pq = p_q[d2 % 2]
                pairs = [(wq_s[:, c * D + d2 * 128: c * D + d2 * 128 + 128],
                          tT[:, c * 512:(c + 1) * 512]) for c in range(8)]
                mm = e_mm_acc(kb, pq[:], pairs,
                              deps=amax(tok_wq, evs_all,
                                        bank_free.get(pq.name)))
                dst = qT[:, d2 * 512:(d2 + 1) * 512]
                tok3 = rope_chunk(dst, pq[:], d2, own_start + 512 * qb, 512,
                                  d2 % 2,
                                  amax(mm, bank_free.get("qTw")))
                bank_free[pq.name] = kb.tok("act")
            qT_done = kb.tok("dve")
            mcol = stats[:, 16:20]
            lcol = stats[:, 20:24]
            i1 = kb.op("dve", lambda v: v.memset(mcol, -3e38), deps=(qT_done,),
                       tick="dve")
            i2 = kb.op("dve", lambda v: v.memset(lcol, 0.0), tick="dve")
            i3 = kb.op("dve", lambda v: v.memset(ob[:], 0.0),
                       deps=amax(bank_free.get("ob")), tick="dve")
            init_tok = amax(i1, i2, i3)
            ncap = (own_start + 512 * qb + 512) // 512
            load_kv(qb, 0)
            if ncap > 1:
                load_kv(qb, 1)
            o_upd = {}
            for kbi in range(ncap):
                par = kbi % 2
                kv_last = []
                for qi in range(4):
                    q_abs = own_start + 512 * qb + 128 * qi
                    valid = min(512, q_abs + 128 - 512 * kbi)
                    if valid <= 0:
                        continue
                    nsub = (valid + 127) // 128
                    ps = p_s[qi % 2]
                    pairs = [(qT[:, c * 512 + qi * 128:c * 512 + qi * 128 + 128],
                              kTb[:, par * 4096 + c * 512:
                                  par * 4096 + c * 512 + valid])
                             for c in range(8)]
                    mm = e_mm_acc(kb, ps[:, 0:valid], pairs,
                                  deps=amax(kld[kbi], vld[kbi], qT_done,
                                            bank_free.get(ps.name)))
                    diag = (512 * kbi + valid == q_abs + 128)
                    if diag:
                        mm = kb.op("dve", lambda v, ps=ps, valid=valid:
                                   v.tensor_tensor(
                                       ps[:, valid - 128:valid],
                                       ps[:, valid - 128:valid], trimask[:],
                                       ALU.add),
                                   deps=amax(mm, tok_tm), tick="dve")
                    sc = stats[:, 24 + 8 * (qi % 2):24 + 8 * (qi % 2) + 8]
                    t1 = kb.op("dve", lambda v, ps=ps, sc=sc, valid=valid:
                               v.reduce_max(sc[:, 0:1], ps[:, 0:valid],
                                            axis=AX.X),
                               deps=amax(mm, init_tok), tick="dve")
                    t1b = kb.op("dve", lambda v, sc=sc: v.tensor_scalar_mul(
                        sc[:, 0:1], sc[:, 0:1], SCALE), deps=(t1,),
                        tick="dve")
                    t2 = kb.op("dve", lambda v, sc=sc, qi=qi: v.tensor_max(
                        sc[:, 1:2], mcol[:, qi:qi + 1], sc[:, 0:1]),
                        deps=amax(t1b, o_upd.get(qi)), tick="dve")
                    t3 = kb.op("dve", lambda v, sc=sc: v.tensor_scalar_mul(
                        sc[:, 2:3], sc[:, 1:2], -1.0), deps=(t2,), tick="dve")
                    pb_ = pbuf[:, (qi % 2) * 512:(qi % 2) * 512 + 512]
                    ex = kb.op("act", lambda s, pb_=pb_, ps=ps, sc=sc,
                               valid=valid: s.activation(
                                   pb_[:, 0:valid], ps[:, 0:valid], AF.Exp,
                                   scale=SCALE, bias=sc[:, 2:3],
                                   accum_out=sc[:, 3:4]),
                               deps=amax(t3, bank_free.get(("pb", qi % 2))),
                               tick="act")
                    bank_free[ps.name] = ex
                    if valid < 512:
                        zz = kb.op("dve", lambda v, pb_=pb_, valid=valid:
                                   v.memset(pb_[:, valid:512], 0.0),
                                   deps=amax(bank_free.get(("pbz", qi % 2))),
                                   tick="dve")
                    else:
                        zz = None
                    a1 = kb.op("dve", lambda v, sc=sc, qi=qi: v.tensor_sub(
                        sc[:, 4:5], mcol[:, qi:qi + 1], sc[:, 1:2]),
                        deps=(t2,), tick="dve")
                    a2 = kb.op("act", lambda s, sc=sc: s.activation(
                        sc[:, 4:5], sc[:, 4:5], AF.Exp), deps=(a1,),
                        tick="act")
                    a3 = kb.op("dve", lambda v, sc=sc, qi=qi:
                               v.scalar_tensor_tensor(
                                   lcol[:, qi:qi + 1], lcol[:, qi:qi + 1],
                                   sc[:, 4:5], sc[:, 3:4], ALU.mult, ALU.add),
                               deps=amax(a2, ex), tick="dve")
                    a4 = kb.op("dve", lambda v, sc=sc, qi=qi: v.tensor_copy(
                        mcol[:, qi:qi + 1], sc[:, 1:2]), deps=amax(a3),
                        tick="dve")
                    pt = p_t[qi % 2]
                    srcs = [pb_[:, 128 * c:128 * (c + 1)] for c in range(nsub)]
                    aT3 = bass.AP(tbf, (qi % 2) * D,
                                  [[2 * D, 128], [128, nsub], [1, 128]])
                    evt = e_transpose4(kb, srcs, ident[:], pt, aT3,
                                       deps=amax(ex, zz, tok_id),
                                       bank_free=bank_free.get(pt.name))
                    bank_free[pt.name] = evt
                    bank_free[("pb", qi % 2)] = evt
                    bank_free[("pbz", qi % 2)] = evt
                    for half in range(2):
                        pv = p_v[half]
                        prs = [(tbf[:, (qi % 2) * D + c * 128:
                                    (qi % 2) * D + c * 128 + 128],
                                vbb[:, par * 4 * D + c * D + half * 512:
                                    par * 4 * D + c * D + half * 512 + 512])
                               for c in range(nsub)]
                        mmv = e_mm_acc(kb, pv[:], prs,
                                       deps=amax(evt, vld[kbi],
                                                 bank_free.get(pv.name)))
                        kv_last.append(mmv)
                        ou = kb.op("dve", lambda v, pv=pv, qi=qi, half=half,
                                   sc=sc: v.scalar_tensor_tensor(
                                       ob[:, qi * D + half * 512:
                                          qi * D + half * 512 + 512],
                                       ob[:, qi * D + half * 512:
                                          qi * D + half * 512 + 512],
                                       sc[:, 4:5], pv[:], ALU.mult, ALU.add),
                                   deps=amax(mmv, a2, init_tok), tick="dve")
                        bank_free[pv.name] = ou
                        o_upd[qi] = amax(o_upd.get(qi), ou)
                bank_free[("kv", par)] = amax(*kv_last) if kv_last else None
                if kbi + 2 < ncap:
                    load_kv(qb, kbi + 2)
            st_toks = []
            for qi in range(4):
                rc = stats[:, 32 + qi:33 + qi]
                r1 = kb.op("dve", lambda v, rc=rc, qi=qi: v.reciprocal(
                    rc, lcol[:, qi:qi + 1]), deps=amax(o_upd.get(qi)),
                    tick="dve")
                f1 = kb.op("dve", lambda v, rc=rc, qi=qi: v.tensor_scalar(
                    ob[:, qi * D:(qi + 1) * D], ob[:, qi * D:(qi + 1) * D],
                    rc, None, ALU.mult), deps=(r1,), tick="dve")
                f2 = kb.op("dve", lambda v, qi=qi: v.tensor_tensor(
                    ob[:, qi * D:(qi + 1) * D], ob[:, qi * D:(qi + 1) * D],
                    xb[:, qi * D:(qi + 1) * D], ALU.add), deps=(f1,),
                    tick="dve")
                s1 = kb.dma("sp", xout[512 * qb + 128 * qi:
                                       512 * qb + 128 * (qi + 1), :],
                            ob[:, qi * D:(qi + 1) * D], deps=(f2,),
                            sem="xstore")
                st_toks.append(f2)
            x_last_read[("B", qb)] = amax(*st_toks)
            bank_free["ob"] = kb.tok("xstore")
            bank_free["qTw"] = kb.tok("pe")

        kb.op("sp", lambda e: e.nop(), deps=(kb.tok("xstore"),))

        with ExitStack() as st2:
            kb.alloc_sems(st2)
            with nc.Block() as block:
                kb.finalize(block)
    return kb


def _rope_tables():
    pos = np.arange(SEQ, dtype=np.float64)[:, None]
    inv = 1.0 / (10000.0 ** (np.arange(0, D, 2, dtype=np.float64) / D))
    fr = pos * inv
    cos = np.repeat(np.cos(fr), 2, axis=-1)
    sin = np.repeat(np.sin(fr), 2, axis=-1)
    sgn = np.ones(D); sgn[0::2] = -1.0
    sinp = sin * sgn
    return (np.ascontiguousarray(cos.T.astype(np.float32)),
            np.ascontiguousarray(sinp.T.astype(np.float32)))


def _perm_pairswap():
    p = np.zeros((128, 128), dtype=ml_dtypes.bfloat16)
    for i in range(0, 128, 2):
        p[i, i + 1] = 1.0
        p[i + 1, i] = 1.0
    return p


def _trimask():
    m = np.zeros((128, 128), dtype=np.float32)
    for r in range(128):
        m[r, r + 1:] = -1e30
    return m


def run_block3(inputs, xcur):
    cosT, sinT = _rope_tables()
    perm = _perm_pairswap()
    trim = _trimask()
    x2 = np.asarray(xcur, np.float32)
    common = {
        "WqT": np.ascontiguousarray(np.asarray(inputs["L0_attn_b_wq"],
                                               np.float32).T),
        "WkT": np.ascontiguousarray(np.asarray(inputs["L0_attn_b_wk"],
                                               np.float32).T),
        "WvT": np.ascontiguousarray(np.asarray(inputs["L0_attn_b_wv"],
                                               np.float32).T),
        "cosT": cosT, "sinT": sinT, "perm": perm, "trimask": trim,
        "ident": _ident128}
    out = np.empty((4, 2, TPC, D), dtype=np.float32)
    for hf in range(2):
        nc = bass.Bass("TRN2", target_bir_lowering=False)
        build_block3(nc, own_start=hf * TPC)
        nkey = hf * TPC + TPC
        in_maps = []
        cores = [2 * b + hf for b in range(4)]
        for b in range(4):
            in_maps.append(dict(
                common,
                xfull=np.ascontiguousarray(x2[b, 0:nkey]),
                xown=np.ascontiguousarray(x2[b, hf * TPC:(hf + 1) * TPC])))
        res = run_bass_kernel_spmd(nc, in_maps, core_ids=cores)
        for b in range(4):
            out[b, hf] = res.results[b]["xout"]
    return out.reshape(4, SEQ, D)



LW = 64
QKS = 8.0


def build_l0attn(nc, paranoid=False, pset=()):
    kb = KB(nc, paranoid)
    _pset = set(pset)

    def _ph(name):
        if ("%s:pe" % name) in _pset:
            kb.paranoid = "pe"
        elif ("%s:dma" % name) in _pset:
            kb.paranoid = "dma"
        elif ("%s:dve" % name) in _pset:
            kb.paranoid = "dve"
        elif ("%s:act" % name) in _pset:
            kb.paranoid = "act"
        elif ("%s:dveact" % name) in _pset:
            kb.paranoid = "dveact"
        else:
            kb.paranoid = paranoid or (name in _pset)
    dt = nc.dram_tensor
    NTIN = 4480
    tin = dt("tin", [NTIN, D], F32, kind="ExternalInput").ap()
    Wqkv = dt("Wqkv", [D, 3 * D], F32, kind="ExternalInput").ap()
    Wo = dt("Wo", [D, D], F32, kind="ExternalInput").ap()
    csq_t = dt("csq_t", [NTIN, LW], F32, kind="ExternalInput").ap()
    ssq_t = dt("ssq_t", [NTIN, LW], F32, kind="ExternalInput").ap()
    csk_t = dt("csk_t", [NTIN, LW], F32, kind="ExternalInput").ap()
    ssk_t = dt("ssk_t", [NTIN, LW], F32, kind="ExternalInput").ap()
    qs_rep = dt("qs_rep", [128, LW], F32, kind="ExternalInput").ap()
    qss_rep = dt("qss_rep", [128, LW], F32, kind="ExternalInput").ap()
    ks_rep = dt("ks_rep", [128, LW], F32, kind="ExternalInput").ap()
    kss_rep = dt("kss_rep", [128, LW], F32, kind="ExternalInput").ap()
    masks_d = dt("masks_d", [32, 128, 384], F32, kind="ExternalInput").ap()
    ident_d = dt("ident", [128, 128], BF16, kind="ExternalInput").ap()
    tout = dt("tout", [TPC, D], F32, kind="ExternalOutput").ap()

    st = ExitStack()
    with st:
        def sb(name, shape, dtype):
            return st.enter_context(nc.sbuf_tensor(name, shape, dtype))

        def psum(name, shape, dtype=F32):
            return st.enter_context(nc.psum_tensor(name, shape, dtype))

        wqkv_s = sb("wqkv", [128, 8 * 3 * D], BF16)
        wo_s = sb("wo", [128, 8 * D], BF16)
        ident = sb("ident_s", [128, 128], BF16)
        qsr = sb("qsr", [128, 4 * LW], F32)
        shb = sb("shb", [128, 4 * D], BF16)
        xb = sb("xb", [128, 4 * D], F32)
        shT = sb("shT", [128, 8 * 512], BF16)
        qrotb = sb("qrotb", [128, 2 * D], BF16)
        krotb = sb("krotb", [128, 2 * D], BF16)
        qTr = sb("qTr", [128, 8 * 512], BF16)
        kTr = sb("kTr", [128, 8 * 1024], BF16)
        vring = sb("vring", [128, 8 * D], BF16)
        ctxb = sb("ctxb", [128, 2 * D], BF16)
        ctxT = sb("ctxT", [128, 8 * 512], BF16)
        pbufs = sb("pbufs", [128, 2 * 384], BF16)
        pts = sb("pts", [128, 2 * 384], BF16)
        tabs = sb("tabs", [128, 2 * 4 * LW], F32)
        scrf = sb("scrf", [128, 2 * 1024], F32)
        maskb = sb("maskb", [128, 2 * 384], F32)
        lbuf = sb("lbuf", [128, 2 * NHEADS], F32)
        outb = sb("outb", [128, D], F32)
        stats = sb("stats", [128, 64], F32)

        p_t = [psum("p_t0", [128, 512], BF16), psum("p_t1", [128, 512], BF16)]
        p_q = [psum("p_q0", [128, 512]), psum("p_q1", [128, 512])]
        p_s = [psum("p_s0", [128, 512]), psum("p_s1", [128, 512])]
        p_c = [psum("p_c0", [128, 512]), psum("p_c1", [128, 512])]
        bank_free = {}

        def wload(dst2d, src, nch):
            sap = src.rearrange("(c p) d -> p c d", p=128)
            return kb.dma("pool", grouped(dst2d, nch), sap, sem="wload")

        wload(wqkv_s[:], Wqkv, 8)
        tok_w = wload(wo_s[:], Wo, 8)
        tok_wo = tok_w
        kb.dma("sp", ident[:], ident_d, sem="cload")
        kb.dma("sp", qsr[:, 0:LW], qs_rep, sem="cload")
        kb.dma("sp", qsr[:, LW:2 * LW], qss_rep, sem="cload")
        kb.dma("sp", qsr[:, 2 * LW:3 * LW], ks_rep, sem="cload")
        tok_qs = kb.dma("sp", qsr[:, 3 * LW:4 * LW], kss_rep, sem="cload")
        tok_id = tok_qs

        def qk_epilogue(tile_gi, par, which, dst_bf, ps_half, half, mm_tok):
            toff = par * 4 * LW + (0 if which == 0 else 2 * LW)
            cs = tabs[:, toff:toff + LW]
            ss = tabs[:, toff + LW:toff + 2 * LW]
            if half == 0:
                src_c = csq_t if which == 0 else csk_t
                src_s = ssq_t if which == 0 else ssk_t
                lc = kb.dma("sp", cs,
                            src_c[tile_gi * 128:(tile_gi + 1) * 128, :],
                            deps=amax(bank_free.get(("tabw", par, which))),
                            sem="tl%d%d" % (par, which))
                ls = kb.dma("sp", ss,
                            src_s[tile_gi * 128:(tile_gi + 1) * 128, :],
                            deps=amax(bank_free.get(("tabw", par, which))),
                            sem="tl%d%d" % (par, which))
                qv = qsr[:, 0:LW] if which == 0 else qsr[:, 2 * LW:3 * LW]
                qv2 = qsr[:, LW:2 * LW] if which == 0 else \
                    qsr[:, 3 * LW:4 * LW]
                m1 = kb.op("dve", lambda v: v.tensor_tensor(
                    cs, cs, qv, ALU.mult), deps=amax(lc, ls, tok_qs),
                    tick="dve")
                m2 = kb.op("dve", lambda v: v.tensor_tensor(
                    ss, ss, qv2, ALU.mult), deps=amax(ls, m1), tick="dve")
                bank_free[("tabp", par, which)] = m2
            tab_tok = bank_free[("tabp", par, which)]
            sq = scrf[:, par * 1024:par * 1024 + 512]
            a = scrf[:, par * 1024 + 512:par * 1024 + 1024]
            ps3 = grouped(ps_half[:], 8)
            sq3 = grouped(sq, 8)
            a3 = grouped(a, 8)
            ssq = stats[:, 16 + 16 * par + 8 * which:
                        16 + 16 * par + 8 * which + 8]
            t1 = kb.op("act", lambda s: s.activation(sq, ps_half[:],
                                                     AF.Square),
                       deps=amax(mm_tok, bank_free.get(("sq", par))),
                       tick="act")
            t2 = kb.op("dve", lambda v: v.reduce_sum(
                grouped(ssq, 8), sq3, axis=AX.X), deps=(t1,), tick="dve")
            t3 = kb.op("act", lambda s: s.activation(ssq, ssq, AF.Sqrt),
                       deps=(t2,), tick="act")
            t4 = kb.op("dve", lambda v: v.tensor_scalar_max(ssq, ssq, 1e-12),
                       deps=(t3,), tick="dve")
            t5 = kb.op("dve", lambda v: v.reciprocal(ssq, ssq), deps=(t4,),
                       tick="dve")
            rn3 = grouped(ssq, 8).broadcast_to([128, 8, LW])
            cs3 = bass.AP(tabs, toff, [[8 * LW, 128], [0, 8], [1, LW]])
            ss3a = bass.AP(tabs, toff + LW, [[8 * LW, 128], [0, 8], [1, 32]])
            ss3b = bass.AP(tabs, toff + LW + 32,
                           [[8 * LW, 128], [0, 8], [1, 32]])
            u1 = kb.op("dve", lambda v: v.tensor_tensor(a3, ps3, cs3,
                                                        ALU.mult),
                       deps=amax(tab_tok, mm_tok, t1,
                                 bank_free.get(("a", par))), tick="dve")
            u2 = kb.op("dve", lambda v: v.tensor_tensor(a3, a3, rn3,
                                                        ALU.mult),
                       deps=amax(u1, t5), tick="dve")
            swA = ps_half[:].rearrange("p (h e) -> p h e", h=8)[:, :, 32:64]
            swB = ps_half[:].rearrange("p (h e) -> p h e", h=8)[:, :, 0:32]
            sq3a = sq3[:, :, 0:32]
            sq3b = sq3[:, :, 32:64]
            v1 = kb.op("dve", lambda v: v.tensor_tensor(sq3a, swA, ss3a,
                                                        ALU.mult),
                       deps=amax(tab_tok, t2), tick="dve")
            v2 = kb.op("dve", lambda v: v.tensor_tensor(sq3b, swB, ss3b,
                                                        ALU.mult),
                       deps=(v1,), tick="dve")
            v3 = kb.op("dve", lambda v: v.tensor_tensor(sq3, sq3, rn3,
                                                        ALU.mult),
                       deps=amax(v2, t5), tick="dve")
            v4 = kb.op("dve", lambda v: v.tensor_tensor(
                grouped(dst_bf, 8), a3, sq3, ALU.add),
                deps=amax(u2, v3, bank_free.get(("qkdst", which, par, half))),
                tick="dve")
            bank_free[("sq", par)] = v4
            bank_free[("a", par)] = v4
            bank_free[("tabw", par, which)] = amax(
                bank_free.get(("tabw", par, which)), v4)
            return v4

        kr_done = {}
        vr_done = {}
        kr_read = {}
        vr_read = {}

        def kv_tile(si, gi_t, slot, par, shT_done):
            ev_toks = []
            for half in range(2):
                pq = p_q[half]
                pairs = [(shT[:, d2 * 512 + si * 128:d2 * 512 + si * 128 + 128],
                          wqkv_s[:, d2 * 3 * D + 2 * D + half * 512:
                                 d2 * 3 * D + 2 * D + half * 512 + 512])
                         for d2 in range(8)]
                mm = e_mm_acc(kb, pq[:], pairs,
                              deps=amax(tok_w, shT_done,
                                        bank_free.get(pq.name)))
                ev = kb.op("act", lambda s, pq=pq, half=half: s.copy(
                    vring[:, slot * D + half * 512:slot * D + half * 512 + 512],
                    pq[:]), deps=amax(mm, vr_read.get(slot)), tick="act")
                bank_free[pq.name] = ev
                ev_toks.append(ev)
            vr_done[slot] = amax(*ev_toks)
            k_toks = []
            for half in range(2):
                pq = p_q[half]
                pairs = [(shT[:, d2 * 512 + si * 128:d2 * 512 + si * 128 + 128],
                          wqkv_s[:, d2 * 3 * D + D + half * 512:
                                 d2 * 3 * D + D + half * 512 + 512])
                         for d2 in range(8)]
                mm = e_mm_acc(kb, pq[:], pairs,
                              deps=amax(tok_w, shT_done,
                                        bank_free.get(pq.name)))
                dst = krotb[:, par * D + half * 512:par * D + half * 512 + 512]
                tk = qk_epilogue(gi_t, par, 1, dst, pq, half, mm)
                bank_free[pq.name] = amax(kb.tok("dve"), kb.tok("act"))
                k_toks.append(tk)
            evs = []
            for g in range(2):
                pt = p_t[g]
                srcs = [krotb[:, par * D + 128 * (4 * g + c):
                              par * D + 128 * (4 * g + c + 1)]
                        for c in range(4)]
                dstap = bass.AP(kTr, (4 * g) * 1024 + slot * 128,
                                [[8 * 1024, 128], [1024, 4], [1, 128]])
                ev = e_transpose4(kb, srcs, ident[:], pt, dstap,
                                  deps=amax(*k_toks) + amax(tok_id),
                                  bank_free=bank_free.get(pt.name),
                                  evac_war=kr_read.get(slot))
                bank_free[pt.name] = ev
                evs.append(ev)
            bank_free[("qkdst", 1, par, 0)] = evs[-1]
            bank_free[("qkdst", 1, par, 1)] = evs[-1]
            kr_done[slot] = amax(*evs)

        ld_pre = []
        for j, gi in enumerate((1, 2)):
            r0 = gi * 128
            d1 = kb.dma("pool", shb[:, j * D:j * D + 512], tin[r0:r0 + 128,
                                                               0:512],
                        sem="xload")
            d2 = kb.dma("pool", shb[:, j * D + 512:(j + 1) * D],
                        tin[r0 - 1:r0 + 127, 512:1024], sem="xload")
            ld_pre += [d1, d2]
        pre_evs = []
        for j in range(2):
            for g in range(2):
                pt = p_t[g]
                srcs = [shb[:, j * D + 128 * (4 * g + c):
                            j * D + 128 * (4 * g + c + 1)] for c in range(4)]
                dstap = bass.AP(shT, (4 * g) * 512 + j * 128,
                                [[8 * 512, 128], [512, 4], [1, 128]])
                ev = e_transpose4(kb, srcs, ident[:], pt, dstap,
                                  deps=amax(*ld_pre) + amax(tok_id),
                                  bank_free=bank_free.get(pt.name))
                bank_free[pt.name] = ev
                pre_evs.append(ev)
        pre_done = amax(*pre_evs)
        for j, gi in enumerate((1, 2)):
            kv_tile(j, gi, gi % 8, j % 2, pre_done)
        bank_free["shb"] = kb.tok("pe")

        for b in range(8):
            _ph("load")
            r0 = 384 + 512 * b
            ld_sh = []
            for i in range(4):
                d1 = kb.dma("pool", shb[:, i * D:i * D + 512],
                            tin[r0 + 128 * i:r0 + 128 * (i + 1), 0:512],
                            deps=amax(bank_free.get("shb")), sem="xload")
                d2 = kb.dma("pool", shb[:, i * D + 512:(i + 1) * D],
                            tin[r0 + 128 * i - 1:r0 + 128 * (i + 1) - 1,
                                512:1024],
                            deps=amax(bank_free.get("shb")), sem="xload")
                ld_sh += [d1, d2]
            xl = kb.dma("sp", grouped(xb[:], 4),
                        tin[r0:r0 + 512, :].rearrange("(a p) d -> p a d",
                                                      p=128),
                        deps=amax(bank_free.get("xbr")), sem="xres")
            ld_tok = amax(*ld_sh)
            evs_all = []
            for i in range(4):
                for g in range(2):
                    pt = p_t[g]
                    srcs = [shb[:, i * D + 128 * (4 * g + c):
                                i * D + 128 * (4 * g + c + 1)]
                            for c in range(4)]
                    dstap = bass.AP(shT, (4 * g) * 512 + i * 128,
                                    [[8 * 512, 128], [512, 4], [1, 128]])
                    ev = e_transpose4(kb, srcs, ident[:], pt, dstap,
                                      deps=amax(ld_tok, tok_id),
                                      bank_free=bank_free.get(pt.name))
                    bank_free[pt.name] = ev
                    evs_all.append(ev)
            shT_done = amax(*evs_all)
            bank_free["shb"] = shT_done

            qT_evs = []
            for i in range(4):
                a_idx = 4 * b + i
                gi_t = a_idx + 3
                slot = gi_t % 8
                par = i % 2
                _ph("qmm")
                q_toks = []
                for half in range(2):
                    pq = p_q[half]
                    pairs = [(shT[:, d2 * 512 + i * 128:
                                  d2 * 512 + i * 128 + 128],
                              wqkv_s[:, d2 * 3 * D + half * 512:
                                     d2 * 3 * D + half * 512 + 512])
                             for d2 in range(8)]
                    mm = e_mm_acc(kb, pq[:], pairs,
                                  deps=amax(tok_w, shT_done,
                                            bank_free.get(pq.name)))
                    dst = qrotb[:, par * D + half * 512:
                                par * D + half * 512 + 512]
                    tq = qk_epilogue(gi_t, par, 0, dst, pq, half, mm)
                    bank_free[pq.name] = amax(kb.tok("dve"), kb.tok("act"))
                    q_toks.append(tq)
                _ph("qtr")
                for g in range(2):
                    pt = p_t[g]
                    srcs = [qrotb[:, par * D + 128 * (4 * g + c):
                                  par * D + 128 * (4 * g + c + 1)]
                            for c in range(4)]
                    dstap = bass.AP(qTr, (4 * g) * 512 + i * 128,
                                    [[8 * 512, 128], [512, 4], [1, 128]])
                    ev = e_transpose4(kb, srcs, ident[:], pt, dstap,
                                      deps=amax(*q_toks) + amax(tok_id),
                                      bank_free=bank_free.get(pt.name),
                                      evac_war=bank_free.get(("qTw", i)))
                    bank_free[pt.name] = ev
                    bank_free[("qkdst", 0, par, 0)] = ev
                    bank_free[("qkdst", 0, par, 1)] = ev
                    qT_evs.append(ev)
                _ph("kv")
                kv_tile(i, gi_t, slot, par, shT_done)
                _ph("none")
            qT_done = amax(*qT_evs)

            _ph("attn")
            for qi in range(4):
                a_idx = 4 * b + qi
                mpar = qi % 2
                ml = kb.dma("sp", maskb[:, mpar * 384:mpar * 384 + 384],
                            masks_d[a_idx],
                            deps=amax(bank_free.get(("mask", mpar))),
                            sem="ml%d" % mpar)
                for h in range(NHEADS):
                    h2, hh = h // 2, h % 2
                    ps = p_s[h % 2]
                    lhs_q = qTr[hh * 64:hh * 64 + 64,
                                h2 * 512 + qi * 128:h2 * 512 + qi * 128 + 128]
                    mm3 = None
                    kdeps = amax(qT_done, bank_free.get(ps.name),
                                 *[kr_done.get((a_idx + 1 + w) % 8)
                                   for w in range(3)])
                    for w in range(3):
                        slot_w = (a_idx + 1 + w) % 8
                        rhs = kTr[hh * 64:hh * 64 + 64,
                                  h2 * 1024 + slot_w * 128:
                                  h2 * 1024 + slot_w * 128 + 128]
                        mm3 = kb.op(
                            "pe", lambda t, ps=ps, lhs_q=lhs_q, rhs=rhs, w=w:
                            t.matmul(ps[:, w * 128:(w + 1) * 128], lhs_q, rhs,
                                     start=(w == 0), stop=(w == 2),
                                     skip_group_check=True),
                            deps=kdeps if w == 0 else (),
                            tick="pe" if w == 2 else None)
                    for w in range(3):
                        kr_read[(a_idx + 1 + w) % 8] = amax(
                            kr_read.get((a_idx + 1 + w) % 8), mm3)
                    mk = kb.op("dve", lambda v, ps=ps, mpar=mpar:
                               v.tensor_tensor(ps[:, 0:384], ps[:, 0:384],
                                               maskb[:, mpar * 384:
                                                     mpar * 384 + 384],
                                               ALU.add),
                               deps=amax(mm3, ml), tick="dve")
                    sc = stats[:, 48 + 4 * (h % 2):48 + 4 * (h % 2) + 4]
                    s1 = kb.op("dve", lambda v, ps=ps, sc=sc: v.reduce_max(
                        sc[:, 0:1], ps[:, 0:384], axis=AX.X), deps=(mk,),
                        tick="dve")
                    s2 = kb.op("dve", lambda v, sc=sc: v.tensor_scalar(
                        sc[:, 1:2], sc[:, 0:1], -QKS, None, ALU.mult),
                        deps=(s1,), tick="dve")
                    pb_ = pbufs[:, (h % 2) * 384:(h % 2) * 384 + 384]
                    ex = kb.op("act", lambda s, pb_=pb_, ps=ps, sc=sc:
                               s.activation(pb_, ps[:, 0:384], AF.Exp,
                                            scale=QKS, bias=sc[:, 1:2],
                                            accum_out=sc[:, 2:3]),
                               deps=amax(s2, bank_free.get(("pbw", h % 2))),
                               tick="act")
                    bank_free[ps.name] = ex
                    rl = kb.op("dve", lambda v, sc=sc, h=h, mpar=mpar:
                               v.reciprocal(lbuf[:, mpar * NHEADS + h:
                                                 mpar * NHEADS + h + 1],
                                            sc[:, 2:3]), deps=(ex,),
                               tick="dve")
                    pt = p_t[h % 2]
                    srcs = [pb_[:, 128 * c:128 * (c + 1)] for c in range(3)]
                    dstap = bass.AP(pts, (h % 2) * 384,
                                    [[2 * 384, 128], [128, 3], [1, 128]])
                    evt = e_transpose4(kb, srcs, ident[:], pt, dstap,
                                       deps=amax(ex, tok_id),
                                       bank_free=bank_free.get(pt.name),
                                       evac_war=bank_free.get(("ptw", h % 2)))
                    bank_free[pt.name] = evt
                    bank_free[("pbw", h % 2)] = evt
                    pcb = p_c[h // 8]
                    col = (h % 8) * 64
                    mmv = None
                    vdeps = amax(evt, bank_free.get(pcb.name)
                                 if h % 8 == 0 else None,
                                 *[vr_done.get((a_idx + 1 + w) % 8)
                                   for w in range(3)])
                    for w in range(3):
                        slot_w = (a_idx + 1 + w) % 8
                        lhsP = pts[:, (h % 2) * 384 + w * 128:
                                   (h % 2) * 384 + w * 128 + 128]
                        rhsV = vring[:, slot_w * D + h * LW:
                                     slot_w * D + h * LW + LW]
                        mmv = kb.op(
                            "pe", lambda t, pcb=pcb, lhsP=lhsP, rhsV=rhsV,
                            col=col, fb=(h % 8 == 0 and w == 0): t.matmul(
                                pcb[:, col:col + LW], lhsP, rhsV,
                                start=fb, stop=False,
                                skip_group_check=True),
                            deps=vdeps if w == 0 else (),
                            tick="pe" if w == 2 else None)
                    for w in range(3):
                        vr_read[(a_idx + 1 + w) % 8] = amax(
                            vr_read.get((a_idx + 1 + w) % 8), mmv)
                    bank_free[("ptw", h % 2)] = mmv
                    bank_free[("pvmm", h // 8)] = mmv
                ct_evs = []
                for j in range(2):
                    pcb = p_c[j]
                    rl3 = grouped(lbuf[:, mpar * NHEADS + j * 8:
                                       mpar * NHEADS + j * 8 + 8],
                                  8).broadcast_to([128, 8, LW])
                    dst3 = grouped(ctxb[:, mpar * D + j * 512:
                                        mpar * D + j * 512 + 512], 8)
                    ev = kb.op("dve", lambda v, dst3=dst3, pcb=pcb, rl3=rl3:
                               v.tensor_tensor(dst3, grouped(pcb[:], 8), rl3,
                                               ALU.mult),
                               deps=amax(bank_free.get(("pvmm", j)),
                                         kb.tok("dve") if False else None,
                                         bank_free.get(("ctxw", mpar, j))),
                               tick="dve")
                    bank_free[pcb.name] = ev
                    ct_evs.append(ev)
                for g in range(2):
                    pt = p_t[g]
                    srcs = [ctxb[:, mpar * D + 128 * (4 * g + c):
                                 mpar * D + 128 * (4 * g + c + 1)]
                            for c in range(4)]
                    dstap = bass.AP(ctxT, (4 * g) * 512 + qi * 128,
                                    [[8 * 512, 128], [512, 4], [1, 128]])
                    ev = e_transpose4(kb, srcs, ident[:], pt, dstap,
                                      deps=amax(*ct_evs) + amax(tok_id),
                                      bank_free=bank_free.get(pt.name),
                                      evac_war=bank_free.get(("cTw", qi)))
                    bank_free[pt.name] = ev
                    bank_free[("ctxw", mpar, g)] = ev
                    bank_free[("cTw", qi)] = ev
                bank_free[("mask", mpar)] = kb.tok("dve")
            ctxT_done = kb.tok("act")

            _ph("out")
            for i in range(4):
                res_toks = []
                for half in range(2):
                    pq = p_q[half]
                    pairs = [(ctxT[:, h2 * 512 + i * 128:
                                   h2 * 512 + i * 128 + 128],
                              wo_s[:, h2 * D + half * 512:
                                   h2 * D + half * 512 + 512])
                             for h2 in range(8)]
                    mm = e_mm_acc(kb, pq[:], pairs,
                                  deps=amax(tok_wo, ctxT_done,
                                            bank_free.get(pq.name)))
                    tr = kb.op("dve", lambda v, pq=pq, half=half, i=i:
                               v.tensor_tensor(
                                   outb[:, half * 512:half * 512 + 512],
                                   pq[:], xb[:, i * D + half * 512:
                                             i * D + half * 512 + 512],
                                   ALU.add),
                               deps=amax(mm, bank_free.get("outb"), xl),
                               tick="dve")
                    bank_free[pq.name] = tr
                    res_toks.append(tr)
                kb.dma("sp", tout[512 * b + 128 * i:512 * b + 128 * (i + 1),
                                  :], outb[:], deps=amax(*res_toks),
                       sem="xstore")
                bank_free["outb"] = kb.tok("xstore")
                bank_free["xbr"] = amax(*res_toks)
            pe_now = kb.tok("pe")
            for i in range(4):
                bank_free[("qTw", i)] = pe_now
                bank_free[("cTw", i)] = pe_now

        kb.op("sp", lambda e: e.nop(), deps=(kb.tok("xstore"),))
        with ExitStack() as st2:
            kb.alloc_sems(st2)
            with nc.Block() as block:
                kb.finalize(block)
    return kb


def build_l0ffn(nc):
    kb = KB(nc)
    dt = nc.dram_tensor
    NTIN = 4224
    FP = 2816
    tin = dt("tin", [NTIN, D], F32, kind="ExternalInput").ap()
    W1p = dt("W1p", [D, 2 * FP], F32, kind="ExternalInput").ap()
    W2p = dt("W2p", [FP, D], F32, kind="ExternalInput").ap()
    b1a = dt("b1a", [128, 22], F32, kind="ExternalInput").ap()
    b1g = dt("b1g", [128, 22], F32, kind="ExternalInput").ap()
    b2rep = dt("b2rep", [128, D], F32, kind="ExternalInput").ap()
    gbrep = dt("gbrep", [128, 2 * D], F32, kind="ExternalInput").ap()
    identf_d = dt("identf", [128, 128], F32, kind="ExternalInput").ap()
    tout = dt("tout", [TPC, D], F32, kind="ExternalOutput").ap()
    w1b_d = dt("w1b_d", [D, 2 * FP], BF16, kind="Internal").ap()
    w2b_d = dt("w2b_d", [FP, D], BF16, kind="Internal").ap()

    st = ExitStack()
    with st:
        def sb(name, shape, dtype):
            return st.enter_context(nc.sbuf_tensor(name, shape, dtype))

        def psum(name, shape, dtype=F32):
            return st.enter_context(nc.psum_tensor(name, shape, dtype))

        w1buf = sb("w1buf", [128, 8 * FP], BF16)
        w2sb = sb("w2sb", [128, 2 * D], BF16)
        bcols = sb("bcols", [128, 64], F32)
        b2b = sb("b2b", [128, D], F32)
        gbb = sb("gbb", [128, 2 * D], F32)
        identf = sb("identf_s", [128, 128], F32)
        shb = sb("shb", [128, 4 * D], F32)
        xb = sb("xb", [128, 4 * D], F32)
        xnb = sb("xnb", [128, 2 * D], F32)
        xnT = sb("xnT", [128, 8 * 512], BF16)
        gT = sb("gT", [128, 11 * 512], BF16)
        gsc = sb("gsc", [128, 2 * 512], F32)
        acc = sb("acc", [128, 4 * D], F32)
        outb = sb("outb", [128, D], F32)
        lnsq = sb("lnsq", [128, D], F32)
        stats = sb("stats", [128, 64], F32)
        konst = sb("konst", [128, 2], F32)

        pb = [psum("pb%d" % i, [128, 512]) for i in range(8)]
        bank_free = {}

        tok_w1c = kb.dma("pool", w1b_d, W1p, sem="wcast")
        tok_w2c = kb.dma("pool", w2b_d, W2p, sem="wcast")
        kb.dma("sp", bcols[:, 0:22], b1a, sem="cload")
        kb.dma("sp", bcols[:, 32:54], b1g, sem="cload")
        kb.dma("sp", b2b[:], b2rep, sem="cload")
        kb.dma("sp", gbb[:], gbrep, sem="cload")
        tok_idf = kb.dma("sp", identf[:], identf_d, sem="cload")
        tok_bc1 = tok_bc2 = tok_b2 = tok_gb = tok_idf
        kb.op("dve", lambda v: v.memset(konst[:, 0:1], 1e-5))
        tok_k = kb.op("dve", lambda v: v.memset(konst[:, 1:2], 0.0),
                      tick="dve")
        eps5 = (konst[:, 0:1], tok_k)

        w1_ld = {}

        def load_w1(ps_, dep):
            d1 = kb.dma("pool", grouped(w1buf[:, 0:8 * 1408], 8)
                        if False else
                        bass.AP(w1buf, 0, [[8 * FP, 128], [FP, 8], [1, 1408]]),
                        w1b_d[:, ps_ * 1408:(ps_ + 1) * 1408].rearrange(
                            "(c p) f -> p c f", p=128),
                        deps=amax(dep, tok_w1c), sem="w1ld")
            d2 = kb.dma("pool",
                        bass.AP(w1buf, 1408, [[8 * FP, 128], [FP, 8],
                                              [1, 1408]]),
                        w1b_d[:, FP + ps_ * 1408:FP + (ps_ + 1) * 1408]
                        .rearrange("(c p) f -> p c f", p=128),
                        deps=amax(dep, tok_w1c), sem="w1ld")
            w1_ld[ps_] = amax(d1, d2)

        load_w1(0, None)
        tbfc_war = {}
        for b in range(8):
            r0 = 128 + 512 * b
            ld_sh = []
            for i in range(4):
                d1 = kb.dma("sp", shb[:, i * D:i * D + 512],
                            tin[r0 + 128 * i:r0 + 128 * (i + 1), 0:512],
                            deps=amax(bank_free.get("shb")), sem="xload")
                d2 = kb.dma("sp", shb[:, i * D + 512:(i + 1) * D],
                            tin[r0 + 128 * i - 1:r0 + 128 * (i + 1) - 1,
                                512:1024],
                            deps=amax(bank_free.get("shb")), sem="xload")
                ld_sh += [d1, d2]
            xl = kb.dma("sp", grouped(xb[:], 4),
                        tin[r0:r0 + 512, :].rearrange("(a p) d -> p a d",
                                                      p=128),
                        deps=amax(bank_free.get("xbr")), sem="xres")
            ld_tok = amax(*ld_sh)
            ev_all = []
            ln_toks = []
            for i in range(4):
                par = i % 2
                sh_ap = shb[:, i * D:(i + 1) * D]
                xn = xnb[:, par * D:par * D + D]
                ln, _ = e_layernorm(kb, sh_ap, D, xn, lnsq[:], stats,
                                    4 * par, eps5, deps=(ld_tok,),
                                    gb=gbb[:, 0:D], bb=gbb[:, D:2 * D],
                                    war_out=bank_free.get(("xn", par)))
                ln_toks.append(ln)
                for g in range(2):
                    pt = pb[6 + g]
                    t = None
                    for c in range(4):
                        s_ap = xn[:, 128 * (4 * g + c):128 * (4 * g + c + 1)]
                        t = kb.op("pe", lambda te, s_ap=s_ap, c=c, pt=pt:
                                  te.transpose(pt[:, 128 * c:128 * (c + 1)],
                                               s_ap, identf[:]),
                                  deps=amax(ln, tok_idf,
                                            bank_free.get(pt.name))
                                  if c == 0 else (),
                                  tick="pe" if c == 3 else None)
                    dstap = bass.AP(xnT, (4 * g) * 512 + i * 128,
                                    [[8 * 512, 128], [512, 4], [1, 128]])
                    ev = kb.op("act", lambda s, dstap=dstap, pt=pt:
                               s.copy(dstap, grouped(pt[:], 4)),
                               deps=amax(t, bank_free.get(("xnTw", i, g))),
                               tick="act")
                    bank_free[pt.name] = ev
                    bank_free[("xnTw", i, g)] = ev
                    ev_all.append(ev)
                bank_free[("xn", par)] = ev_all[-1]
            xnT_done = amax(*ev_all)
            bank_free["shb"] = amax(*ln_toks)
            bank_free["gsc0"] = kb.tok("act")

            acc_toks = {}
            for ps_ in range(2):
                g_toks = {}
                for fi in range(11):
                    pa, pg = pb[6], pb[7]
                    pairs_a = [(w1buf[:, c * FP + fi * 128:
                                      c * FP + fi * 128 + 128],
                                xnT[:, c * 512:(c + 1) * 512])
                               for c in range(8)]
                    pairs_g = [(w1buf[:, c * FP + 1408 + fi * 128:
                                      c * FP + 1408 + fi * 128 + 128],
                                xnT[:, c * 512:(c + 1) * 512])
                               for c in range(8)]
                    mma = e_mm_acc(kb, pa[:], pairs_a,
                                   deps=amax(w1_ld[ps_], xnT_done,
                                             bank_free.get(pa.name)))
                    mmg = e_mm_acc(kb, pg[:], pairs_g,
                                   deps=amax(w1_ld[ps_], xnT_done,
                                             bank_free.get(pg.name)))
                    gs = gsc[:, (fi % 2) * 512:(fi % 2) * 512 + 512]
                    ge = kb.op("act", lambda s, gs=gs, pg=pg, fi=fi, ps_=ps_:
                               s.activation(gs, pg[:], AF.Gelu,
                                            bias=bcols[:, 32 + ps_ * 11 + fi:
                                                       33 + ps_ * 11 + fi]),
                               deps=amax(mmg, tok_bc2,
                                         bank_free.get(("gsc", fi % 2))),
                               tick="act")
                    gm = kb.op("dve", lambda v, gs=gs, pa=pa, fi=fi, ps_=ps_:
                               v.scalar_tensor_tensor(
                                   gT[:, fi * 512:(fi + 1) * 512], pa[:],
                                   bcols[:, ps_ * 11 + fi:ps_ * 11 + fi + 1],
                                   gs, ALU.add, ALU.mult),
                               deps=amax(ge, tok_bc1, mma,
                                         bank_free.get(("gTw", fi))),
                               tick="dve")
                    bank_free[pa.name] = gm
                    bank_free[pg.name] = gm
                    bank_free[("gsc", fi % 2)] = gm
                    g_toks[fi] = gm
                if ps_ == 0:
                    load_w1(1, kb.tok("pe"))
                elif b + 1 < 8:
                    load_w1(0, kb.tok("pe"))
                wmm = {}
                for fi in range(11):
                    fc = ps_ * 11 + fi
                    wb = w2sb[:, (fi % 2) * D:(fi % 2) * D + D]
                    wl = kb.dma("sp", wb, w2b_d[128 * fc:128 * (fc + 1), :],
                                deps=amax(tok_w2c,
                                          bank_free.get(("w2", fi % 2))),
                                sem="w2l%d" % (fi % 2))
                    for tt in range(4):
                        for hf in range(2):
                            pp = pb[tt * 2 + hf]
                            lhs = gT[:, fi * 512 + tt * 128:
                                     fi * 512 + tt * 128 + 128]
                            rhs = wb[:, hf * 512:hf * 512 + 512]
                            dep0 = amax(wl, g_toks[fi],
                                        bank_free.get(pp.name)
                                        if fi == 0 else None)
                            wmm[(tt, hf)] = kb.op(
                                "pe", lambda t, pp=pp, lhs=lhs, rhs=rhs,
                                fi=fi: t.matmul(pp[:], lhs, rhs,
                                                start=(fi == 0),
                                                stop=(fi == 10),
                                                skip_group_check=True),
                                deps=dep0,
                                tick="pe" if (fi == 10 or
                                              (tt == 3 and hf == 1))
                                else None)
                    bank_free[("w2", fi % 2)] = wmm[(3, 1)]
                for fi in range(11):
                    bank_free[("gTw", fi)] = wmm[(3, 1)]
                for tt in range(4):
                    for hf in range(2):
                        pp = pb[tt * 2 + hf]
                        a_ap = acc[:, tt * D + hf * 512:tt * D + hf * 512 + 512]
                        if ps_ == 0:
                            ev = kb.op("act", lambda s, a_ap=a_ap, pp=pp:
                                       s.copy(a_ap, pp[:]),
                                       deps=amax(wmm[(tt, hf)],
                                                 bank_free.get(("accw", tt,
                                                                hf))),
                                       tick="act")
                            acc_toks[(tt, hf)] = ev
                        else:
                            e1 = kb.op("dve", lambda v, a_ap=a_ap, pp=pp:
                                       v.tensor_tensor(a_ap, a_ap, pp[:],
                                                       ALU.add),
                                       deps=amax(wmm[(tt, hf)],
                                                 acc_toks[(tt, hf)]),
                                       tick="dve")
                            e2 = kb.op("dve", lambda v, a_ap=a_ap:
                                       v.tensor_tensor(a_ap, a_ap, b2b[
                                           :, hf * 512:hf * 512 + 512],
                                           ALU.add),
                                       deps=amax(e1, tok_b2), tick="dve")
                            e3 = kb.op("dve", lambda v, a_ap=a_ap, tt=tt,
                                       hf=hf: v.tensor_tensor(
                                           a_ap, a_ap,
                                           xb[:, tt * D + hf * 512:
                                              tt * D + hf * 512 + 512],
                                           ALU.add),
                                       deps=amax(e2, xl), tick="dve")
                            acc_toks[(tt, hf)] = e3
                        bank_free[pp.name] = acc_toks[(tt, hf)]
            st_toks = []
            for tt in range(4):
                s1 = kb.dma("sp", tout[512 * b + 128 * tt:
                                       512 * b + 128 * (tt + 1), :],
                            acc[:, tt * D:(tt + 1) * D],
                            deps=amax(acc_toks[(tt, 0)], acc_toks[(tt, 1)]),
                            sem="xstore")
                st_toks.append(s1)
            accw = kb.tok("xstore")
            for tt in range(4):
                for hf in range(2):
                    bank_free[("accw", tt, hf)] = accw
            bank_free["xbr"] = amax(acc_toks[(3, 0)], acc_toks[(3, 1)],
                                    *[acc_toks[(t_, h_)] for t_ in range(4)
                                      for h_ in range(2)])

        kb.op("sp", lambda e: e.nop(), deps=(kb.tok("xstore"),))
        with ExitStack() as st2:
            kb.alloc_sems(st2)
            with nc.Block() as block:
                kb.finalize(block)
    return kb



def build_lnpass(nc):
    kb = KB(nc)
    dt = nc.dram_tensor
    xin = dt("xin", [TPC, D], F32, kind="ExternalInput").ap()
    tout = dt("tout", [TPC, D], F32, kind="ExternalOutput").ap()
    st = ExitStack()
    with st:
        def sb(name, shape, dtype):
            return st.enter_context(nc.sbuf_tensor(name, shape, dtype))
        xb = sb("xb", [128, 4 * D], F32)
        tb = sb("tb", [128, 4 * D], F32)
        lnsq = sb("lnsq", [128, D], F32)
        stats = sb("stats", [128, 16], F32)
        konst = sb("konst", [128, 2], F32)
        kb.sem("none")
        tok_k = kb.op("dve", lambda v: v.memset(konst[:, 0:1], 1e-5),
                      tick="dve")
        eps5 = (konst[:, 0:1], tok_k)
        bank_free = {}
        for b in range(8):
            xl = kb.dma("sp", grouped(xb[:], 4),
                        xin[512 * b:512 * (b + 1), :].rearrange(
                            "(a p) d -> p a d", p=128),
                        deps=amax(bank_free.get("xb")), sem="xload")
            sts = []
            for i in range(4):
                ln, _ = e_layernorm(kb, xb[:, i * D:(i + 1) * D], D,
                                    tb[:, i * D:(i + 1) * D], lnsq[:], stats,
                                    4 * (i % 2), eps5, deps=(xl,),
                                    war_out=bank_free.get(("tb", i)))
                s1 = kb.dma("sp", tout[512 * b + 128 * i:
                                       512 * b + 128 * (i + 1), :],
                            tb[:, i * D:(i + 1) * D], deps=(ln,),
                            sem="xstore")
                bank_free[("tb", i)] = ln
                sts.append(ln)
            bank_free["xb"] = amax(*sts)
        kb.op("sp", lambda e: e.nop(), deps=(kb.tok("xstore"),))
        with ExitStack() as st2:
            kb.alloc_sems(st2)
            with nc.Block() as block:
                kb.finalize(block)
    return kb


def build_addres(nc):
    kb = KB(nc)
    dt = nc.dram_tensor
    a_in = dt("a_in", [TPC, D], F32, kind="ExternalInput").ap()
    b_in = dt("b_in", [TPC, D], F32, kind="ExternalInput").ap()
    xout = dt("xout", [TPC, D], F32, kind="ExternalOutput").ap()
    st = ExitStack()
    with st:
        def sb(name, shape, dtype):
            return st.enter_context(nc.sbuf_tensor(name, shape, dtype))
        ab = sb("ab", [128, 4 * D], F32)
        bb = sb("bb", [128, 4 * D], F32)
        bank_free = {}
        for b in range(8):
            al = kb.dma("sp", grouped(ab[:], 4),
                        a_in[512 * b:512 * (b + 1), :].rearrange(
                            "(a p) d -> p a d", p=128),
                        deps=amax(bank_free.get("ab")), sem="xload")
            bl = kb.dma("sp", grouped(bb[:], 4),
                        b_in[512 * b:512 * (b + 1), :].rearrange(
                            "(a p) d -> p a d", p=128),
                        deps=amax(bank_free.get("ab")), sem="xload")
            ts_ = []
            for i in range(4):
                t1 = kb.op("dve", lambda v, i=i: v.tensor_tensor(
                    ab[:, i * D:(i + 1) * D], ab[:, i * D:(i + 1) * D],
                    bb[:, i * D:(i + 1) * D], ALU.add), deps=amax(al, bl),
                    tick="dve")
                kb.dma("sp", xout[512 * b + 128 * i:512 * b + 128 * (i + 1),
                                  :], ab[:, i * D:(i + 1) * D], deps=(t1,),
                       sem="xstore")
                ts_.append(t1)
            bank_free["ab"] = amax(*ts_) + amax(kb.tok("xstore"))
        kb.op("sp", lambda e: e.nop(), deps=(kb.tok("xstore"),))
        with ExitStack() as st2:
            kb.alloc_sems(st2)
            with nc.Block() as block:
                kb.finalize(block)
    return kb



_prog_cache = {}


def _get_prog(name, builder):
    if name not in _prog_cache:
        nc = bass.Bass("TRN2", target_bir_lowering=False)
        builder(nc)
        _prog_cache[name] = nc
    return _prog_cache[name]


def _spmd(nc, in_maps, cores=None):
    return run_bass_kernel_spmd(nc, in_maps,
                                core_ids=list(range(NCORES))
                                if cores is None else cores)


def _run_tokpar(nc, in_maps, outkey="xout"):
    res = _spmd(nc, in_maps)
    out = np.empty((4, 2, TPC, D), dtype=np.float32)
    for c in range(NCORES):
        out[c // 2, c % 2] = res.results[c][outkey]
    return out.reshape(4, SEQ, D)



_l0w = {}


def _l0ffn_tables():
    if "tabs" in _l0w:
        return _l0w["tabs"]
    n = SEQ
    half = np.arange(0, LW, 2, dtype=np.float64)
    inv_freq = 1.0 / 10000.0 ** (half / LW)
    t = np.arange(-384, n, dtype=np.float64)
    freqs = t[:, None] * inv_freq[None, :]
    freqs = np.concatenate([freqs, freqs], -1)
    base_scale = (half + 0.4 * LW) / (1.4 * LW)
    power = (t - n // 2) / (256 // 2)
    scale = base_scale[None, :] ** power[:, None]
    scale = np.concatenate([scale, scale], -1)
    cos, sin = np.cos(freqs), np.sin(freqs)
    sgn = np.ones(LW)
    sgn[0:32] = -1.0
    tabs = ((cos * scale).astype(np.float32),
            (sin * scale * sgn).astype(np.float32),
            (cos / scale).astype(np.float32),
            (sin / scale * sgn).astype(np.float32))
    _l0w["tabs"] = tabs
    return tabs


def _l0_masks(own0):
    m = np.zeros((32, 128, 384), dtype=np.float32)
    r = np.arange(128)[:, None]
    j = np.arange(384)[None, :]
    for ti in range(32):
        qpos = own0 + 128 * ti + r
        kpos = own0 + 128 * ti - 256 + j
        vis = (qpos - kpos >= 0) & (qpos - kpos <= 256) & (kpos >= 0)
        m[ti] = np.where(vis, 0.0, -1e30)
    return m


def _l0_weights(inputs):
    if "w" in _l0w:
        return _l0w["w"]
    FP = 2816
    Wqkv = np.ascontiguousarray(inputs["L0_ffn_Wqkv"], dtype=np.float32)
    Wo = np.ascontiguousarray(inputs["L0_ffn_Wo"], dtype=np.float32)
    W1 = np.asarray(inputs["L0_ffn_W1"], np.float32)
    b1 = np.asarray(inputs["L0_ffn_b1"], np.float32)
    W2 = np.asarray(inputs["L0_ffn_W2"], np.float32)
    b2 = np.asarray(inputs["L0_ffn_b2"], np.float32)
    qsc = np.asarray(inputs["L0_ffn_q_scale"], np.float32)
    ksc = np.asarray(inputs["L0_ffn_k_scale"], np.float32)
    lg = np.asarray(inputs["L0_ffn_ln_g"], np.float32)
    lb = np.asarray(inputs["L0_ffn_ln_b"], np.float32)
    per = []
    swap = np.concatenate([np.arange(32, 64), np.arange(0, 32)])
    for i in range(6):
        w1p = np.zeros((D, 2 * FP), dtype=np.float32)
        w1p[:, 0:2730] = W1[i][:, 0:2730]
        w1p[:, FP:FP + 2730] = W1[i][:, 2730:5460]
        w2p = np.zeros((FP, D), dtype=np.float32)
        w2p[0:2730] = W2[i]
        b1a = np.zeros((128, 22), np.float32)
        b1g = np.zeros((128, 22), np.float32)
        ba = np.zeros(FP, np.float32)
        ba[0:2730] = b1[i][0:2730]
        bg = np.zeros(FP, np.float32)
        bg[0:2730] = b1[i][2730:5460]
        for fi in range(22):
            b1a[:, fi] = ba[fi * 128:(fi + 1) * 128]
            b1g[:, fi] = bg[fi * 128:(fi + 1) * 128]
        per.append(dict(
            Wqkv=np.ascontiguousarray(Wqkv[i]),
            Wo=np.ascontiguousarray(Wo[i]),
            W1p=w1p, W2p=w2p, b1a=b1a, b1g=b1g,
            b2rep=np.ascontiguousarray(_rep(b2[i])),
            gbrep=np.ascontiguousarray(
                np.concatenate([_rep(lg[i]), _rep(lb[i])], axis=1)),
            qs_rep=np.ascontiguousarray(_rep(qsc[i])),
            qss_rep=np.ascontiguousarray(_rep(qsc[i][swap])),
            ks_rep=np.ascontiguousarray(_rep(ksc[i])),
            kss_rep=np.ascontiguousarray(_rep(ksc[i][swap])),
        ))
    _l0w["w"] = per
    return per


def run_l0ffn(inputs, xcur):
    x = np.asarray(xcur, np.float32)
    per = _l0_weights(inputs)
    csq, ssq, csk, ssk = _l0ffn_tables()
    nc_ln = _get_prog("lnpass", build_lnpass)
    nc_attn = _get_prog("l0attn", build_l0attn)
    nc_ffn = _get_prog("l0ffn", build_l0ffn)
    nc_add = _get_prog("addres", build_addres)
    masks = {0: _l0_masks(0), 1: _l0_masks(TPC)}

    shards = _shard_tokens(x)
    t = _run_tokpar(nc_ln, [{"xin": np.ascontiguousarray(s)} for s in shards],
                    "tout")
    for i in range(6):
        wl = per[i]
        for kind in ("attn", "ffn"):
            halo = 384 if kind == "attn" else 128
            in_maps = []
            for c in range(NCORES):
                bat, hf = c // 2, c % 2
                own0 = hf * TPC
                tb = t[bat]
                if own0 - halo < 0:
                    tin = np.concatenate(
                        [np.zeros((halo, D), np.float32), tb[0:TPC]], axis=0)
                else:
                    tin = tb[own0 - halo:own0 + TPC]
                m = dict(tin=np.ascontiguousarray(tin))
                if kind == "attn":
                    rows = slice(384 - halo + own0, 384 - halo + own0 + 4480)
                    lo = own0 - 384 + 384
                    m.update(Wqkv=wl["Wqkv"], Wo=wl["Wo"],
                             csq_t=np.ascontiguousarray(csq[lo:lo + 4480]),
                             ssq_t=np.ascontiguousarray(ssq[lo:lo + 4480]),
                             csk_t=np.ascontiguousarray(csk[lo:lo + 4480]),
                             ssk_t=np.ascontiguousarray(ssk[lo:lo + 4480]),
                             qs_rep=wl["qs_rep"], qss_rep=wl["qss_rep"],
                             ks_rep=wl["ks_rep"], kss_rep=wl["kss_rep"],
                             masks_d=masks[hf], ident=_ident128)
                else:
                    m.update(W1p=wl["W1p"], W2p=wl["W2p"], b1a=wl["b1a"],
                             b1g=wl["b1g"], b2rep=wl["b2rep"],
                             gbrep=wl["gbrep"], identf=_identf128)
                in_maps.append(m)
            nc = nc_attn if kind == "attn" else nc_ffn
            t = _run_tokpar(nc, in_maps, "tout")
    in_maps = []
    tsh = _shard_tokens(t)
    for c in range(NCORES):
        in_maps.append({"a_in": np.ascontiguousarray(shards[c]),
                        "b_in": np.ascontiguousarray(tsh[c])})
    return _run_tokpar(nc_add, in_maps, "xout")


DK = 256


def build_block7a(nc, p1n=16, p2n=64, p3n=64, p4n=64, p2m=9):
    kb = KB(nc)
    dt = nc.dram_tensor
    xfull = dt("xfull", [SEQ, D], F32, kind="ExternalInput").ap()
    WqT_s = dt("WqT_s", [D, 512], F32, kind="ExternalInput").ap()
    WkT_s = dt("WkT_s", [D, 512], F32, kind="ExternalInput").ap()
    WvT_s = dt("WvT_s", [D, 512], F32, kind="ExternalInput").ap()
    bmixT = dt("bmixT", [D, 4], F32, kind="ExternalInput").ap()
    convrep = dt("convrep", [128, 3 * 4 * 512], F32, kind="ExternalInput").ap()
    normw_rep = dt("normw_rep", [128, 2 * DK], F32, kind="ExternalInput").ap()
    ident_d = dt("ident", [128, 128], BF16, kind="ExternalInput").ap()
    identf_d = dt("identf", [128, 128], F32, kind="ExternalInput").ap()
    mA_d = dt("mA_d", [128, 128], F32, kind="ExternalInput").ap()
    mC_d = dt("mC_d", [128, 128], F32, kind="ExternalInput").ap()
    oT_out = dt("oT_out", [512, SEQ], BF16, kind="ExternalOutput").ap()
    qraw_d = dt("qraw_d", [SEQ + 4, 512], F32, kind="Internal").ap()
    kraw_d = dt("kraw_d", [SEQ + 4, 512], F32, kind="Internal").ap()
    vraw_d = dt("vraw_d", [SEQ + 4, 512], F32, kind="Internal").ap()
    bm_d = dt("bm_d", [SEQ, 4], F32, kind="Internal").ap()
    qhT_d = dt("qhT_d", [512, SEQ], BF16, kind="Internal").ap()
    khT_d = dt("khT_d", [512, SEQ], BF16, kind="Internal").ap()
    kh_d = dt("kh_d", [SEQ, 512], BF16, kind="Internal").ap()
    u_d = dt("u_d", [SEQ, 512], F32, kind="Internal").ap()
    wT_d = dt("wT_d", [512, SEQ], BF16, kind="Internal").ap()
    vsil_d = dt("vsil_d", [SEQ, 512], BF16, kind="Internal").ap()
    o_d = dt("o_d", [SEQ, 512], F32, kind="Internal").ap()

    st = ExitStack()
    with st:
        def sb(name, shape, dtype):
            return st.enter_context(nc.sbuf_tensor(name, shape, dtype))

        def psum(name, shape, dtype=F32):
            return st.enter_context(nc.psum_tensor(name, shape, dtype))

        wq_s = sb("wq", [128, 8 * 512], BF16)
        wk_s = sb("wk", [128, 8 * 512], BF16)
        wv_s = sb("wv", [128, 8 * 512], BF16)
        wbm_s = sb("wbm", [128, 8 * 4], BF16)
        conv_s = sb("conv", [128, 3 * 4 * 512], F32)
        nw_s = sb("nw", [128, 2 * DK], F32)
        ident = sb("ident_s", [128, 128], BF16)
        identf = sb("identf_s", [128, 128], F32)
        mA = sb("mA_s", [128, 128], F32)
        mC = sb("mC_s", [128, 128], F32)
        xb = sb("xb", [128, 4 * D], F32)
        tb = sb("tb", [128, 4 * D], F32)
        tbf = sb("tbf", [128, 2 * D], BF16)
        tT = sb("tT", [128, 8 * 512], BF16)
        stg = sb("stg", [128, 4 * 512], F32)
        lnsq = sb("lnsq", [128, D], F32)
        shl = sb("shl", [128, 4 * 512], F32)
        cvb = sb("cvb", [128, 2 * 512], F32)
        qkv2 = sb("qkv2", [128, 6 * 512], F32)
        bmt = sb("bmt", [128, 4], F32)
        tokb = sb("tokb", [128, 8 * 512], BF16)
        TTf = sb("TTf", [128, 4 * 128], F32)
        TTb = sb("TTb", [128, 4 * 128], BF16)
        sc_q = sb("sc_q", [128, 2 * 512], BF16)
        sc_k = sb("sc_k", [128, 2 * 512], BF16)
        sc_kh = sb("sc_kh", [128, 2 * 512], BF16)
        sc_u = sb("sc_u", [128, 2 * 512], F32)
        sc_w = sb("sc_w", [128, 2 * 512], BF16)
        Sst = sb("Sst", [128, 2 * 2 * DK], F32)
        Sbf = sb("Sbf", [128, 2 * 2 * DK], BF16)
        aT = sb("aT", [128, 2 * 128], BF16)
        unb = sb("unb", [128, 2 * DK], BF16)
        ob_ = sb("ob", [128, 2 * DK], F32)
        stats = sb("stats", [128, 64], F32)
        konst = sb("konst", [128, 4], F32)

        p_t = [psum("p_t0", [128, 512], BF16), psum("p_t1", [128, 512], BF16)]
        p_q = [psum("p_q0", [128, 512]), psum("p_q1", [128, 512])]
        p_a = [psum("p_a0", [128, 512]), psum("p_a1", [128, 512])]
        p_b = [psum("p_b0", [128, 512]), psum("p_b1", [128, 512])]
        bank_free = {}

        def wload(dst2d, src, nch, w):
            sap = src.rearrange("(c p) d -> p c d", p=128)
            return kb.dma("pool", grouped(dst2d, nch), sap, sem="wload")

        wload(wq_s[:], WqT_s, 8, 512)
        wload(wk_s[:], WkT_s, 8, 512)
        wload(wv_s[:], WvT_s, 8, 512)
        tok_w = wload(wbm_s[:], bmixT, 8, 4)
        kb.dma("sp", conv_s[:], convrep, sem="cload")
        kb.dma("sp", nw_s[:], normw_rep, sem="cload")
        kb.dma("sp", ident[:], ident_d, sem="cload")
        kb.dma("sp", identf[:], identf_d, sem="cload")
        kb.dma("sp", mA[:], mA_d, sem="cload")
        tok_c = kb.dma("sp", mC[:], mC_d, sem="cload")
        kb.op("dve", lambda v: v.memset(konst[:, 0:1], 1e-5))
        kb.op("dve", lambda v: v.memset(konst[:, 1:2], 1e-6))
        tok_k = kb.op("dve", lambda v: v.memset(konst[:, 2:3], 0.0),
                      tick="dve")
        eps5 = (konst[:, 0:1], tok_k)
        zt = kb.op("dve", lambda v: v.memset(stg[:, 0:512], 0.0), tick="dve")
        z1 = kb.dma("sp", qraw_d[0:4, :], stg[0:4, 0:512], deps=(zt,),
                    sem="zst")
        z2 = kb.dma("sp", kraw_d[0:4, :], stg[0:4, 0:512], deps=(zt,),
                    sem="zst")
        z3 = kb.dma("sp", vraw_d[0:4, :], stg[0:4, 0:512], deps=(zt,),
                    sem="zst")

        tbfc_war = {}
        for b in range(p1n):
            xl = kb.dma("sp", grouped(xb[:], 4),
                        xfull[512 * b:512 * (b + 1), :].rearrange(
                            "(a p) d -> p a d", p=128),
                        deps=amax(bank_free.get("xb")), sem="xres")
            evs_all = []
            for i in range(4):
                par = i % 2
                t_f = tb[:, i * D:(i + 1) * D]
                tbfc = tbf[:, par * D:par * D + D]
                ln, bf_tok = e_layernorm(kb, xb[:, i * D:(i + 1) * D], D,
                                         t_f, tbfc, stats, 4 * par, eps5,
                                         deps=(xl,), out_bf=tbfc,
                                         war_sq=tbfc_war.get(par))
                for g in range(2):
                    pt = p_t[g]
                    srcs = [tbfc[:, 128 * (4 * g + c):128 * (4 * g + c + 1)]
                            for c in range(4)]
                    dstap = bass.AP(tT, (4 * g) * 512 + i * 128,
                                    [[8 * 512, 128], [512, 4], [1, 128]])
                    ev = e_transpose4(kb, srcs, ident[:], pt, dstap,
                                      deps=(bf_tok, tok_c),
                                      bank_free=bank_free.get(pt.name))
                    bank_free[pt.name] = ev
                    evs_all.append(ev)
                tbfc_war[par] = evs_all[-1]
            shT_done = amax(*evs_all)
            x_rd = []
            for i in range(4):
                par = i % 2
                for which, wsb, dst_d in ((0, wq_s, qraw_d), (1, wk_s, kraw_d),
                                          (2, wv_s, vraw_d)):
                    pq = p_q[which % 2]
                    pairs = [(tT[:, d2 * 512 + i * 128:d2 * 512 + i * 128 + 128],
                              wsb[:, d2 * 512:(d2 + 1) * 512])
                             for d2 in range(8)]
                    mm = e_mm_acc(kb, pq[:], pairs,
                                  deps=amax(tok_w, shT_done,
                                            bank_free.get(pq.name)))
                    sg = stg[:, which * 512:(which + 1) * 512]
                    ev = kb.op("act", lambda s, sg=sg, pq=pq: s.copy(sg, pq[:]),
                               deps=amax(mm, bank_free.get(("stg", which))),
                               tick="act")
                    bank_free[pq.name] = ev
                    sv = kb.dma("sp", dst_d[4 + 512 * b + 128 * i:
                                            4 + 512 * b + 128 * (i + 1), :],
                                sg, deps=(ev,), sem="rst")
                    bank_free[("stg", which)] = sv
                pq = p_a[0]
                pairs = [(tT[:, d2 * 512 + i * 128:d2 * 512 + i * 128 + 128],
                          wbm_s[:, d2 * 4:(d2 + 1) * 4]) for d2 in range(8)]
                mm = e_mm_acc(kb, pq[:, 0:4], pairs,
                              deps=amax(tok_w, shT_done,
                                        bank_free.get(pq.name)))
                sg = stg[:, 3 * 512:3 * 512 + 4]
                ev = kb.op("act", lambda s, sg=sg, pq=pq: s.activation(
                    sg, pq[:, 0:4], AF.Sigmoid),
                    deps=amax(mm, bank_free.get(("stg", 3))), tick="act")
                bank_free[pq.name] = ev
                sv = kb.dma("sp", bm_d[512 * b + 128 * i:
                                       512 * b + 128 * (i + 1), :], sg,
                            deps=(ev,), sem="rst")
                bank_free[("stg", 3)] = sv
            bank_free["xb"] = shT_done
        p1_done = kb.tok("rst")

        def conv_silu(src_d, ti, coff, out_ap, par, extra=None):
            lds = []
            for tap in range(4):
                dst = shl[:, tap * 512:(tap + 1) * 512]
                lds.append(kb.dma("sp", dst,
                                  src_d[4 + 128 * ti - 3 + tap:
                                        4 + 128 * ti - 3 + tap + 128, :],
                                  deps=amax(bank_free.get(("shl", par))),
                                  sem="sh%d" % par))
            acc = cvb[:, par * 512:(par + 1) * 512]
            t1 = kb.op("dve", lambda v: v.tensor_tensor(
                acc, shl[:, 0:512], conv_s[:, coff:coff + 512], ALU.mult),
                deps=amax(*lds) + amax(p1_done, bank_free.get(("cvb", par))),
                tick="dve")
            for tap in range(1, 4):
                tm = stg[:, par * 512:(par + 1) * 512]
                t2 = kb.op("dve", lambda v, tap=tap, tm=tm: v.tensor_tensor(
                    tm, shl[:, tap * 512:(tap + 1) * 512],
                    conv_s[:, coff + tap * 512:coff + (tap + 1) * 512],
                    ALU.mult), deps=amax(t1, bank_free.get(("cstg", par))),
                    tick="dve")
                t1 = kb.op("dve", lambda v, tm=tm: v.tensor_tensor(
                    acc, acc, tm, ALU.add), deps=(t2,), tick="dve")
                bank_free[("cstg", par)] = t1
            bank_free[("shl", par)] = t1
            sl = kb.op("act", lambda s: s.activation(out_ap, acc, AF.Silu),
                       deps=amax(t1, extra), tick="act")
            bank_free[("cvb", par)] = sl
            return sl

        uw_war = {}
        for ti in range(p2n):
            par = ti % 2
            qh = qkv2[:, 0:512]
            kh = qkv2[:, 512:1024]
            vs = qkv2[:, 1024:1536]
            s_q = conv_silu(qraw_d, ti, 0, qh, par,
                            extra=amax(bank_free.get("qkv2")))
            s_k = conv_silu(kraw_d, ti, 2048, kh, par)
            s_v = conv_silu(vraw_d, ti, 4096, vs, par)
            if p2m < 2:
                bank_free["qkv2"] = amax(s_q, s_k, s_v)
                continue
            bl = kb.dma("sp", bmt[:], bm_d[128 * ti:128 * (ti + 1), :],
                        deps=amax(bank_free.get("bmt")), sem="bml")
            nrm_toks = []
            for which, ap_ in ((0, qh), (1, kh)):
                sq = stg[:, 2 * 512:3 * 512]
                ssq = stats[:, 16 + 8 * which:16 + 8 * which + 2]
                t1 = kb.op("act", lambda s, ap_=ap_, sq=sq: s.activation(
                    sq, ap_, AF.Square),
                    deps=amax(s_q if which == 0 else s_k,
                              bank_free.get("sqs")), tick="act")
                t2 = kb.op("dve", lambda v, sq=sq, ssq=ssq: v.reduce_sum(
                    grouped(ssq, 2), grouped(sq, 2), axis=AX.X), deps=(t1,),
                    tick="dve")
                t3 = kb.op("act", lambda s, ssq=ssq: s.activation(
                    ssq, ssq, AF.Sqrt, bias=konst[:, 1:2]),
                    deps=amax(t2, tok_k), tick="act")
                t4 = kb.op("dve", lambda v, ssq=ssq: v.reciprocal(ssq, ssq),
                           deps=(t3,), tick="dve")
                rn3 = grouped(ssq, 2).broadcast_to([128, 2, DK])
                t5 = kb.op("dve", lambda v, ap_=ap_, rn3=rn3: v.tensor_tensor(
                    grouped(ap_, 2), grouped(ap_, 2), rn3, ALU.mult),
                    deps=(t4,), tick="dve")
                bank_free["sqs"] = t5
                nrm_toks.append(t5)
            qhb = tokb[:, 0:512]
            khb = tokb[:, 512:1024]
            kbb = tokb[:, 1024:1536]
            vbb_ = tokb[:, 1536:2048]
            vsb = tokb[:, 2048:2560]
            c1 = kb.op("dve", lambda v: v.tensor_copy(qhb, qh),
                       deps=amax(nrm_toks[0], bank_free.get("tokb")),
                       tick="dve")
            c2 = kb.op("dve", lambda v: v.tensor_copy(khb, kh),
                       deps=amax(nrm_toks[1], c1), tick="dve")
            c3 = kb.op("dve", lambda v: v.tensor_copy(vsb, vs),
                       deps=amax(s_v, c2), tick="dve")
            bk_toks = []
            for hh in range(2):
                b1 = kb.op("dve", lambda v, hh=hh: v.tensor_scalar(
                    kbb[:, hh * DK:(hh + 1) * DK], kh[:, hh * DK:(hh + 1) * DK],
                    bmt[:, hh:hh + 1], None, ALU.mult),
                    deps=amax(nrm_toks[1], bl, c3), tick="dve")
                b2 = kb.op("dve", lambda v, hh=hh: v.tensor_scalar(
                    vbb_[:, hh * DK:(hh + 1) * DK], vs[:, hh * DK:(hh + 1) * DK],
                    bmt[:, hh:hh + 1], None, ALU.mult),
                    deps=amax(s_v, bl, b1), tick="dve")
                bk_toks.append(b2)
            tok_tokb = amax(c1, c2, c3, *bk_toks)
            bank_free["qkv2"] = tok_tokb
            bank_free["bmt"] = tok_tokb
            sv1 = kb.dma("sp", vsil_d[128 * ti:128 * (ti + 1), :], vsb,
                         deps=(c3,), sem="p2st")
            sv2 = kb.dma("sp", kh_d[128 * ti:128 * (ti + 1), :], khb,
                         deps=(c2,), sem="p2st")
            if p2m < 3:
                continue
            trs = {}
            for nm, srcb in (("q", qhb), ("k", khb), ("kb", kbb)):
                pt = p_t[par]
                srcs = [srcb[:, 128 * c:128 * (c + 1)] for c in range(4)]
                dst3 = bass.AP(tokb, 2560 + {"q": 0, "k": 512, "kb": 1024}[nm],
                               [[8 * 512, 128], [128, 4], [1, 128]])
                ev = e_transpose4(kb, srcs, ident[:], pt, dst3,
                                  deps=amax(tok_tokb, tok_c),
                                  bank_free=bank_free.get(pt.name),
                                  evac_war=bank_free.get(("trw", nm)))
                bank_free[pt.name] = ev
                trs[nm] = ev
            qhT = tokb[:, 2560:2560 + 512]
            khT = tokb[:, 3072:3072 + 512]
            kbT = tokb[:, 3584:3584 + 512]
            sv3 = kb.dma("sp", qhT_d[:, 128 * ti:128 * (ti + 1)].rearrange(
                "(c p) t -> p c t", p=128), grouped(qhT, 4),
                deps=(trs["q"],), sem="p2st")
            sv4 = kb.dma("sp", khT_d[:, 128 * ti:128 * (ti + 1)].rearrange(
                "(c p) t -> p c t", p=128), grouped(khT, 4),
                deps=(trs["k"],), sem="p2st")
            bank_free["tokb"] = amax(sv1, sv2, sv3, sv4)
            if p2m < 4:
                continue
            for hh in range(2):
                pa = p_a[hh]
                mmA = None
                for c2_ in range(2):
                    lhs = bass.AP(tokb, 3072 + hh * 256 + c2_ * 128,
                                  [[8 * 512, 128], [1, 128]])
                    rhs = bass.AP(tokb, 3584 + hh * 256 + c2_ * 128,
                                  [[8 * 512, 128], [1, 128]])
                    mmA = kb.op("pe", lambda t, pa=pa, lhs=lhs, rhs=rhs,
                                c2_=c2_: t.matmul(pa[:, 0:128], lhs, rhs,
                                                  start=(c2_ == 0),
                                                  stop=(c2_ == 1)),
                                deps=amax(trs["k"], trs["kb"],
                                          bank_free.get(pa.name))
                                if c2_ == 0 else (), tick="pe" if c2_ == 1
                                else None)
                Bf = TTf[:, (2 * par + hh) * 128:(2 * par + hh) * 128 + 128]
                Bb = TTb[:, (2 * par + hh) * 128:(2 * par + hh) * 128 + 128]
                m1 = kb.op("dve", lambda v, Bf=Bf, pa=pa: v.tensor_tensor(
                    Bf, pa[:, 0:128], mA[:], ALU.mult),
                    deps=amax(mmA, tok_c, bank_free.get(("TT", par, hh))),
                    tick="dve")
                bank_free[pa.name] = m1
                Sf = stg[:, (2 * par + hh) * 128 + 2048 - 2048:0] \
                    if False else None
                Sfb = shl[:, (2 * par + hh) * 128:(2 * par + hh) * 128 + 128]
                m2 = kb.op("dve", lambda v, Sfb=Sfb, Bf=Bf: v.tensor_copy(
                    Sfb, Bf), deps=amax(m1, bank_free.get(("Sfb", par, hh))),
                    tick="dve")
                m3 = kb.op("dve", lambda v, Bf=Bf: v.tensor_tensor(
                    Bf, Bf, identf[:], ALU.add), deps=(m2,), tick="dve")
                ttok = m3
                stok = m2
                sq_t = None
                for stp in range(6 if p2m >= 5 else 0):
                    sb_b = TTb[:, (2 * (1 - par) + hh) * 128:
                               (2 * (1 - par) + hh) * 128 + 128]
                    cb = kb.op("dve", lambda v, sb_b=sb_b, Sfb=Sfb:
                               v.tensor_copy(sb_b, Sfb), deps=amax(stok),
                               tick="dve")
                    pt = p_t[par]
                    tr = kb.op("pe", lambda t, sb_b=sb_b, pt=pt: t.transpose(
                        pt[:, 0:128], sb_b, ident[:]),
                        deps=amax(cb, bank_free.get(pt.name)), tick="pe")
                    sT = tokb[:, 2560 + 1536 + 0:0] if False else None
                    sTb = cvb[:, par * 512:par * 512 + 128].bitcast(BF16) \
                        if False else None
                    sTb = aT[:, hh * 128:(hh + 1) * 128]
                    ev = kb.op("act", lambda s, sTb=sTb, pt=pt: s.copy(
                        sTb, pt[:, 0:128]), deps=amax(tr,
                                                      bank_free.get(("aT",
                                                                     hh))),
                        tick="act")
                    bank_free[pt.name] = ev
                    pb_ = p_b[hh]
                    mmS = kb.op("pe", lambda t, pb_=pb_, sTb=sTb, sb_b=sb_b:
                                t.matmul(pb_[:, 0:128], sTb, sb_b,
                                         start=True, stop=True),
                                deps=amax(ev, bank_free.get(pb_.name),
                                          bank_free.get((pb_.name, 1))),
                                tick="pe")
                    snb = sb_b
                    e1 = kb.op("dve", lambda v, snb=snb, pb_=pb_:
                               v.tensor_copy(snb, pb_[:, 0:128]),
                               deps=amax(mmS), tick="dve")
                    e2 = kb.op("dve", lambda v, Sfb=Sfb, pb_=pb_:
                               v.tensor_copy(Sfb, pb_[:, 0:128]),
                               deps=amax(mmS, e1), tick="dve")
                    stok = e2
                    bank_free[pb_.name] = e2
                    ttb = TTb[:, (2 * par + hh) * 128:
                              (2 * par + hh) * 128 + 128]
                    c4 = kb.op("dve", lambda v, ttb=ttb, Bf=Bf: v.tensor_copy(
                        ttb, Bf), deps=amax(ttok, e1), tick="dve")
                    tr2 = kb.op("pe", lambda t, ttb=ttb, pt=pt: t.transpose(
                        pt[:, 0:128], ttb, ident[:]),
                        deps=amax(c4, bank_free.get(pt.name)), tick="pe")
                    ev2 = kb.op("act", lambda s, sTb=sTb, pt=pt: s.copy(
                        sTb, pt[:, 0:128]), deps=amax(tr2, mmS), tick="act")
                    bank_free[pt.name] = ev2
                    bank_free[("aT", hh)] = ev2
                    mmT = kb.op("pe", lambda t, pb_=pb_, sTb=sTb, snb=snb:
                                t.matmul(pb_[:, 128:256], sTb, snb,
                                         start=True, stop=True),
                                deps=amax(ev2, e1, e2,
                                          bank_free.get((pb_.name, 1))),
                                tick="pe")
                    a1 = kb.op("dve", lambda v, Bf=Bf, pb_=pb_:
                               v.tensor_tensor(Bf, Bf, pb_[:, 128:256],
                                               ALU.add),
                               deps=amax(mmT, c4), tick="dve")
                    ttok = a1
                    bank_free[(pb_.name, 1)] = a1
                bank_free[("aT", hh)] = kb.tok("act")
                if p2m < 6:
                    bank_free[("TT", par, hh)] = ttok
                    bank_free[("Sfb", par, hh)] = stok
                    continue
                ttb = TTb[:, (2 * par + hh) * 128:(2 * par + hh) * 128 + 128]
                cf = kb.op("dve", lambda v, ttb=ttb, Bf=Bf: v.tensor_copy(
                    ttb, Bf), deps=amax(ttok), tick="dve")
                bank_free[("TT", par, hh)] = cf
                bank_free[("Sfb", par, hh)] = stok
                pu = p_a[hh]
                
                mmU = kb.op("pe", lambda t, pu=pu, ttb=ttb, hh=hh: t.matmul(
                    pu[:, 0:DK], ttb,
                    bass.AP(tokb, 1536 + hh * DK, [[8 * 512, 128], [1, DK]]),
                    start=True, stop=False, skip_group_check=True),
                    deps=amax(cf, tok_tokb, bank_free.get(pu.name)),
                    tick="pe")
                mmW = kb.op("pe", lambda t, pu=pu, ttb=ttb, hh=hh: t.matmul(
                    pu[:, DK:2 * DK], ttb,
                    bass.AP(tokb, 1024 + hh * DK, [[8 * 512, 128], [1, DK]]),
                    start=False, stop=True, skip_group_check=True),
                    deps=(), tick="pe")
                ue = kb.op("act", lambda s, pu=pu, hh=hh: s.copy(
                    stg[:, 1536:1536 + DK], pu[:, 0:DK]),
                    deps=amax(mmU, mmW, bank_free.get("ue")), tick="act")
                su = kb.dma("sp", u_d[128 * ti:128 * (ti + 1),
                                      hh * DK:(hh + 1) * DK],
                            stg[:, 1536:1536 + DK], deps=(ue,), sem="p2st")
                bank_free["ue"] = su
                if p2m < 7:
                    bank_free[pu.name] = amax(su, mmW)
                    continue
                wb_ = tokb[:, 1024 + hh * DK:1024 + (hh + 1) * DK]
                we = kb.op("act", lambda s, wb_=wb_, pu=pu: s.copy(
                    wb_, pu[:, DK:2 * DK]),
                    deps=amax(mmW, trs["kb"], ue), tick="act")
                bank_free[pu.name] = amax(we, su)
                if p2m < 8:
                    continue
                pt = p_t[par]
                srcs = [wb_[:, 128 * c:128 * (c + 1)] for c in range(2)]
                dst3 = bass.AP(tokb, 3584 + hh * 256,
                               [[8 * 512, 128], [128, 2], [1, 128]])
                evw = e_transpose4(kb, srcs, ident[:], pt, dst3,
                                   deps=amax(we, tok_c),
                                   bank_free=bank_free.get(pt.name),
                                   evac_war=bank_free.get(("trw", "kb")))
                bank_free[pt.name] = evw
                sw = kb.dma("sp", wT_d[hh * DK:(hh + 1) * DK,
                                       128 * ti:128 * (ti + 1)].rearrange(
                    "(c p) t -> p c t", p=128),
                    grouped(tokb[:, 3584 + hh * 256:3584 + hh * 256 + 256], 2),
                    deps=(evw,), sem="p2st")
                bank_free[("trw", "kb")] = sw
            bank_free[("trw", "q")] = kb.tok("p2st")
            bank_free[("trw", "k")] = kb.tok("p2st")
        p2_done = kb.tok("p2st")

        iz = kb.op("dve", lambda v: v.memset(Sst[:], 0.0),
                   deps=(p2_done,), tick="dve")
        iz2 = kb.op("dve", lambda v: v.memset(Sbf[:], 0.0), tick="dve")
        s_tok = {0: amax(iz, iz2), 1: amax(iz, iz2)}
        ld = {}

        def p3_load(ci):
            par = ci % 2
            dd = []
            dd.append(kb.dma("sp", grouped(sc_q[:, par * 256:par * 256 + 512]
                                           if False else
                                           sc_q[:, par * 512:par * 512 + 512],
                                           4),
                             qhT_d[:, 128 * ci:128 * (ci + 1)].rearrange(
                                 "(c p) t -> p c t", p=128),
                             deps=amax(bank_free.get(("p3", par))),
                             sem="p3l%d" % par))
            dd.append(kb.dma("sp", grouped(sc_k[:, par * 512:par * 512 + 512],
                                           4),
                             khT_d[:, 128 * ci:128 * (ci + 1)].rearrange(
                                 "(c p) t -> p c t", p=128),
                             deps=amax(bank_free.get(("p3", par))),
                             sem="p3l%d" % par))
            dd.append(kb.dma("sp", sc_kh[:, par * 512:par * 512 + 512],
                             kh_d[128 * ci:128 * (ci + 1), :],
                             deps=amax(bank_free.get(("p3", par))),
                             sem="p3l%d" % par))
            dd.append(kb.dma("sp", sc_u[:, par * 512:par * 512 + 512],
                             u_d[128 * ci:128 * (ci + 1), :],
                             deps=amax(bank_free.get(("p3", par))),
                             sem="p3l%d" % par))
            dd.append(kb.dma("sp", grouped(sc_w[:, par * 512:par * 512 + 512],
                                           4),
                             wT_d[:, 128 * ci:128 * (ci + 1)].rearrange(
                                 "(c p) t -> p c t", p=128),
                             deps=amax(bank_free.get(("p3", par))),
                             sem="p3l%d" % par))
            ld[ci] = amax(*dd)

        p3_load(0)
        p3_load(1)
        for ci in range(p3n):
            par = ci % 2
            rd_toks = []
            for hh in range(2):
                qT_ap = bass.AP(sc_q, par * 512 + hh * 256,
                                [[2 * 512, 128], [128, 2], [1, 128]])
                kT_ap = bass.AP(sc_k, par * 512 + hh * 256,
                                [[2 * 512, 128], [128, 2], [1, 128]])
                wT_ap = bass.AP(sc_w, par * 512 + hh * 256,
                                [[2 * 512, 128], [128, 2], [1, 128]])
                pa = p_a[hh]
                mmA = None
                for c2_ in range(2):
                    lhs = bass.AP(sc_k, par * 512 + hh * 256 + c2_ * 128,
                                  [[2 * 512, 128], [1, 128]])
                    rhs = bass.AP(sc_q, par * 512 + hh * 256 + c2_ * 128,
                                  [[2 * 512, 128], [1, 128]])
                    mmA = kb.op("pe", lambda t, pa=pa, lhs=lhs, rhs=rhs,
                                c2_=c2_: t.matmul(pa[:, 0:128], lhs, rhs,
                                                  start=(c2_ == 0),
                                                  stop=(c2_ == 1)),
                                deps=amax(ld[ci], bank_free.get(pa.name))
                                if c2_ == 0 else (),
                                tick="pe" if c2_ == 1 else None)
                aT_ap = aT[:, hh * 128:(hh + 1) * 128]
                ma = kb.op("dve", lambda v, aT_ap=aT_ap, pa=pa:
                           v.tensor_tensor(aT_ap, pa[:, 0:128], mC[:],
                                           ALU.mult),
                           deps=amax(mmA, tok_c, bank_free.get(("aTw", hh))),
                           tick="dve")
                bank_free[pa.name] = ma
                pb_ = p_b[hh]
                mmW = None
                for c2_ in range(2):
                    lhs = bass.AP(sc_w, par * 512 + hh * 256 + c2_ * 128,
                                  [[2 * 512, 128], [1, 128]])
                    rhs = Sbf[:, (2 * hh + c2_) * DK:(2 * hh + c2_ + 1) * DK]
                    mmW = kb.op("pe", lambda t, pb_=pb_, lhs=lhs, rhs=rhs,
                                c2_=c2_: t.matmul(pb_[:, 0:DK], lhs, rhs,
                                                  start=(c2_ == 0),
                                                  stop=(c2_ == 1)),
                                deps=amax(ld[ci], s_tok[hh],
                                          bank_free.get(pb_.name))
                                if c2_ == 0 else (),
                                tick="pe" if c2_ == 1 else None)
                un_ap = unb[:, hh * DK:(hh + 1) * DK]
                u_ap = sc_u[:, par * 512 + hh * DK:par * 512 + (hh + 1) * DK]
                us = kb.op("dve", lambda v, un_ap=un_ap, u_ap=u_ap, pb_=pb_:
                           v.tensor_tensor(un_ap, u_ap, pb_[:, 0:DK],
                                           ALU.subtract),
                           deps=amax(mmW, ld[ci], bank_free.get(("unw", hh))),
                           tick="dve")
                bank_free[pb_.name] = us
                po = p_b[hh]
                mo = None
                for c2_ in range(2):
                    lhs = bass.AP(sc_q, par * 512 + hh * 256 + c2_ * 128,
                                  [[2 * 512, 128], [1, 128]])
                    rhs = Sbf[:, (2 * hh + c2_) * DK:(2 * hh + c2_ + 1) * DK]
                    mo = kb.op("pe", lambda t, po=po, lhs=lhs, rhs=rhs, c2_=c2_:
                               t.matmul(po[:, DK:2 * DK], lhs, rhs,
                                        start=(c2_ == 0), stop=False,
                                        skip_group_check=True),
                               deps=amax(us, s_tok[hh]) if c2_ == 0 else (),
                               tick=None)
                mo = kb.op("pe", lambda t, po=po, aT_ap=aT_ap, un_ap=un_ap:
                           t.matmul(po[:, DK:2 * DK], aT_ap, un_ap,
                                    start=False, stop=True,
                                    skip_group_check=True),
                           deps=amax(ma, us), tick="pe")
                bank_free[("aTw", hh)] = mo
                oe = kb.op("act", lambda s, hh=hh, po=po: s.copy(
                    ob_[:, hh * DK:(hh + 1) * DK], po[:, DK:2 * DK]),
                    deps=amax(mo, bank_free.get(("ob", hh))), tick="act")
                bank_free[po.name] = amax(us, oe)
                so = kb.dma("sp", o_d[128 * ci:128 * (ci + 1),
                                      hh * DK:(hh + 1) * DK],
                            ob_[:, hh * DK:(hh + 1) * DK], deps=(oe,),
                            sem="ost")
                bank_free[("ob", hh)] = so
                ps_ = p_a[hh]
                smm = {}
                for c2_ in range(2):
                    lhs = sc_kh[:, par * 512 + hh * DK + c2_ * 128:
                                par * 512 + hh * DK + c2_ * 128 + 128]
                    smm[c2_] = kb.op(
                        "pe", lambda t, ps_=ps_, lhs=lhs, un_ap=un_ap, c2_=c2_:
                        t.matmul(ps_[:, c2_ * DK:(c2_ + 1) * DK],
                                 lhs, un_ap, start=(c2_ == 0),
                                 stop=(c2_ == 1), skip_group_check=True),
                        deps=amax(us, ma, bank_free.get((ps_.name, c2_))),
                        tick="pe")
                sadds = []
                for c2_ in range(2):
                    sa = kb.op("dve", lambda v, hh=hh, c2_=c2_, ps_=ps_:
                               v.tensor_tensor(
                                   Sst[:, (2 * hh + c2_) * DK:
                                       (2 * hh + c2_ + 1) * DK],
                                   Sst[:, (2 * hh + c2_) * DK:
                                       (2 * hh + c2_ + 1) * DK],
                                   ps_[:, c2_ * DK:(c2_ + 1) * DK],
                                   ALU.add), deps=amax(smm[c2_], mmW, mo),
                               tick="dve")
                    sb_ = kb.op("dve", lambda v, hh=hh, c2_=c2_:
                                v.tensor_copy(
                                    Sbf[:, (2 * hh + c2_) * DK:
                                        (2 * hh + c2_ + 1) * DK],
                                    Sst[:, (2 * hh + c2_) * DK:
                                        (2 * hh + c2_ + 1) * DK]),
                                deps=(sa,), tick="dve")
                    sadds.append(sb_)
                    bank_free[(ps_.name, c2_)] = sb_
                s_tok[hh] = amax(*sadds)
                bank_free[ps_.name] = amax(*sadds)
                bank_free[("unw", hh)] = amax(*[smm[c] for c in smm], mo)
                rd_toks.append(s_tok[hh])
            bank_free[("p3", par)] = amax(*rd_toks)
            if ci + 2 < p3n:
                p3_load(ci + 2)
        p3_done = kb.tok("ost")

        for ti in range(p4n):
            par = ti % 2
            olap = stg[:, 0:512]
            vlap = tokb[:, 0:512]
            lo = kb.dma("sp", olap, o_d[128 * ti:128 * (ti + 1), :],
                        deps=amax(p3_done, bank_free.get("p4o")), sem="p4l")
            lv = kb.dma("sp", vlap, vsil_d[128 * ti:128 * (ti + 1), :],
                        deps=amax(bank_free.get("p4o")), sem="p4l")
            lb = kb.dma("sp", bmt[:], bm_d[128 * ti:128 * (ti + 1), :],
                        deps=amax(bank_free.get("p4b")), sem="p4b2")
            toks = []
            for hh in range(2):
                o_h = olap[:, hh * DK:(hh + 1) * DK]
                v_h = vlap[:, hh * DK:(hh + 1) * DK]
                d1 = kb.op("dve", lambda v, o_h=o_h, v_h=v_h: v.tensor_tensor(
                    o_h, o_h, v_h, ALU.subtract), deps=amax(lo, lv),
                    tick="dve")
                d2 = kb.op("dve", lambda v, o_h=o_h, v_h=v_h, hh=hh:
                           v.scalar_tensor_tensor(o_h, o_h,
                                                  bmt[:, 2 + hh:3 + hh], v_h,
                                                  ALU.mult, ALU.add),
                           deps=amax(d1, lb), tick="dve")
                sq = cvb[:, 0:DK]
                ssq = stats[:, 32 + 4 * hh:32 + 4 * hh + 1]
                t1 = kb.op("act", lambda s, sq=sq, o_h=o_h, ssq=ssq:
                           s.activation(sq, o_h, AF.Square, accum_out=ssq),
                           deps=amax(d2, bank_free.get("p4sq")), tick="act")
                bank_free["p4sq"] = t1
                t2 = kb.op("act", lambda s, ssq=ssq: s.activation(
                    ssq, ssq, AF.Sqrt, scale=1.0 / DK, bias=konst[:, 0:1]),
                    deps=amax(t1, tok_k), tick="act")
                t3 = kb.op("dve", lambda v, ssq=ssq: v.reciprocal(ssq, ssq),
                           deps=(t2,), tick="dve")
                t4 = kb.op("dve", lambda v, o_h=o_h, ssq=ssq: v.tensor_scalar(
                    o_h, o_h, ssq, None, ALU.mult), deps=(t3,), tick="dve")
                t5 = kb.op("dve", lambda v, o_h=o_h, hh=hh: v.tensor_tensor(
                    grouped(o_h, 1), grouped(nw_s[:, hh * DK:(hh + 1) * DK], 1)
                    if False else o_h, nw_s[:, 0:DK], ALU.mult),
                    deps=(t4,), tick="dve")
                toks.append(t5)
            cb = kb.op("dve", lambda v: v.tensor_copy(vlap, olap),
                       deps=amax(*toks, lv), tick="dve")
            pt = p_t[par]
            srcs = [vlap[:, 128 * c:128 * (c + 1)] for c in range(4)]
            dst3 = oT_out[:, 128 * ti:128 * (ti + 1)].rearrange(
                "(c p) t -> p c t", p=128)
            stg3 = grouped(tokb[:, 512:1024], 4)
            ev = e_transpose4(kb, srcs, ident[:], pt, stg3,
                              deps=amax(cb, tok_c),
                              bank_free=bank_free.get(pt.name),
                              evac_war=bank_free.get("p4stg"))
            bank_free[pt.name] = ev
            so = kb.dma("sp", dst3, stg3, deps=(ev,), sem="xstore")
            bank_free["p4stg"] = so
            bank_free["p4o"] = amax(cb, ev)
            bank_free["p4b"] = amax(*toks)

        kb.op("sp", lambda e: e.nop(), deps=(kb.tok("xstore"),))
        with ExitStack() as st2:
            kb.alloc_sems(st2)
            with nc.Block() as block:
                kb.finalize(block)
    return kb


def build_block7b(nc):
    kb = KB(nc)
    dt = nc.dram_tensor
    oT = dt("oT", [D, TPC], BF16, kind="ExternalInput").ap()
    WoT2 = dt("WoT2", [D, D], F32, kind="ExternalInput").ap()
    xc = dt("xc", [TPC, D], F32, kind="ExternalInput").ap()
    xout = dt("xout", [TPC, D], F32, kind="ExternalOutput").ap()
    st = ExitStack()
    with st:
        def sb(name, shape, dtype):
            return st.enter_context(nc.sbuf_tensor(name, shape, dtype))

        def psum(name, shape, dtype=F32):
            return st.enter_context(nc.psum_tensor(name, shape, dtype))
        wo_s = sb("wo", [128, 8 * D], BF16)
        otb = sb("otb", [128, 2 * 8 * 512], BF16)
        xb = sb("xb", [128, 4 * D], F32)
        outb = sb("outb", [128, D], F32)
        p_q = [psum("p_q0", [128, 512]), psum("p_q1", [128, 512])]
        bank_free = {}
        tok_w = kb.dma("pool", grouped(wo_s[:], 8),
                       WoT2.rearrange("(c p) d -> p c d", p=128), sem="wload")
        for b in range(8):
            par = b % 2
            ol = kb.dma("sp", grouped(otb[:, par * 4096:par * 4096 + 4096], 8),
                        oT[:, 512 * b:512 * (b + 1)].rearrange(
                            "(c p) t -> p c t", p=128),
                        deps=amax(bank_free.get(("ot", par))), sem="ol%d" % par)
            xl = kb.dma("sp", grouped(xb[:], 4),
                        xc[512 * b:512 * (b + 1), :].rearrange(
                            "(a p) d -> p a d", p=128),
                        deps=amax(bank_free.get("xb")), sem="xres")
            res_toks = []
            for i in range(4):
                for half in range(2):
                    pq = p_q[half]
                    pairs = [(otb[:, par * 4096 + c * 512 + i * 128:
                                  par * 4096 + c * 512 + i * 128 + 128],
                              wo_s[:, c * D + half * 512:
                                   c * D + half * 512 + 512])
                             for c in range(8)]
                    mm = e_mm_acc(kb, pq[:], pairs,
                                  deps=amax(tok_w, ol,
                                            bank_free.get(pq.name)))
                    tr = kb.op("dve", lambda v, pq=pq, half=half, i=i:
                               v.tensor_tensor(
                                   outb[:, half * 512:half * 512 + 512],
                                   pq[:], xb[:, i * D + half * 512:
                                             i * D + half * 512 + 512],
                                   ALU.add),
                               deps=amax(mm, xl, bank_free.get("outb")),
                               tick="dve")
                    bank_free[pq.name] = tr
                    res_toks.append(tr)
                kb.dma("sp", xout[512 * b + 128 * i:512 * b + 128 * (i + 1),
                                  :], outb[:], deps=amax(res_toks[-2],
                                                         res_toks[-1]),
                       sem="xstore")
                bank_free["outb"] = kb.tok("xstore")
            bank_free[("ot", par)] = amax(*res_toks)
            bank_free["xb"] = amax(*res_toks)
        kb.op("sp", lambda e: e.nop(), deps=(kb.tok("xstore"),))
        with ExitStack() as st2:
            kb.alloc_sems(st2)
            with nc.Block() as block:
                kb.finalize(block)
    return kb


def run_block7(inputs, xcur):
    x6 = np.asarray(xcur, np.float32)
    nc7a = _get_prog("b7a", build_block7a)
    nc7b = _get_prog("b7b", build_block7b)
    Wq = np.asarray(inputs["L1_attn_b_Wq"], np.float32)
    Wk = np.asarray(inputs["L1_attn_b_Wk"], np.float32)
    Wv = np.asarray(inputs["L1_attn_b_Wv"], np.float32)
    cq = np.asarray(inputs["L1_attn_b_conv_q"], np.float32)
    ck = np.asarray(inputs["L1_attn_b_conv_k"], np.float32)
    cv = np.asarray(inputs["L1_attn_b_conv_v"], np.float32)
    Wb = np.asarray(inputs["L1_attn_b_Wb"], np.float32)
    Wmix = np.asarray(inputs["L1_attn_b_Wmix"], np.float32)
    nw = np.asarray(inputs["L1_attn_b_norm_w"], np.float32)
    Wo = np.asarray(inputs["L1_attn_b_Wo"], np.float32)
    mA = np.zeros((128, 128), np.float32)
    mC = np.zeros((128, 128), np.float32)
    for j in range(128):
        mA[j, j + 1:] = -1.0
        mC[j, j:] = 1.0
    in_maps = []
    for c in range(NCORES):
        bat, hf = c // 2, c % 2
        sl = slice(512 * hf, 512 * (hf + 1))
        crep = np.zeros((128, 3 * 4 * 512), np.float32)
        for pi, cw in enumerate((cq, ck, cv)):
            cws = cw[sl]
            for tap in range(4):
                crep[:, (pi * 4 + tap) * 512:(pi * 4 + tap + 1) * 512] = \
                    np.broadcast_to(cws[:, tap], (128, 512))
        bmixT = np.ascontiguousarray(
            np.stack([Wb[2 * hf], Wb[2 * hf + 1],
                      Wmix[2 * hf], Wmix[2 * hf + 1]], axis=1))
        in_maps.append(dict(
            xfull=np.ascontiguousarray(x6[bat]),
            WqT_s=np.ascontiguousarray(Wq[sl].T),
            WkT_s=np.ascontiguousarray(Wk[sl].T),
            WvT_s=np.ascontiguousarray(Wv[sl].T),
            bmixT=bmixT, convrep=crep,
            normw_rep=np.ascontiguousarray(
                np.concatenate([_rep(nw), _rep(nw)], axis=1)),
            ident=_ident128, identf=_identf128, mA_d=mA, mC_d=mC))
    res = _spmd(nc7a, in_maps)
    oT = np.empty((4, D, SEQ), dtype=ml_dtypes.bfloat16)
    for c in range(NCORES):
        bat, hf = c // 2, c % 2
        oT[bat, 512 * hf:512 * (hf + 1)] = res.results[c]["oT_out"]
    shards = _shard_tokens(x6)
    WoT2 = np.ascontiguousarray(Wo.T)
    in_maps = []
    for c in range(NCORES):
        bat, hf = c // 2, c % 2
        in_maps.append(dict(
            oT=np.ascontiguousarray(oT[bat][:, 4096 * hf:4096 * (hf + 1)]),
            WoT2=WoT2, xc=np.ascontiguousarray(shards[c])))
    return _run_tokpar(nc7b, in_maps, "xout")
